# version 12
# baseline (speedup 1.0000x reference)
import numpy as np
import contextlib
import concourse.bass as bass
import concourse.mybir as mybir
from concourse.bass_utils import run_bass_kernel_spmd

F32 = mybir.dt.float32
I32 = mybir.dt.int32
AF = mybir.ActivationFunctionType
ALU = mybir.AluOpType
AX = mybir.AxisListType

NCORES = 8
L = 8192
D = 2048
NST = 64
RNK = 128
PER = 512
TLOC = L // NCORES
NTT = TLOC // 128
DSL = D // NCORES
E = 8
IFF = 1408
CAP = 384
TRASH = E * CAP
NSLOT = E * CAP + 128
FD = 1024
NCH = L // FD
PRJ = RNK + 2 * NST
BROW = 260
EPS = 1e-5


class Rec:
    ENGS = ("sync", "tensor", "vector", "scalar", "gpsimd")

    def __init__(self, nc, stack):
        self.nc = nc
        self.stack = stack
        self.q = {e: [] for e in self.ENGS}
        self.semh = {}
        self.semv = {}
        self.lastw = {e: {} for e in self.ENGS}
        for s in ("tc", "vc", "sc", "gc", "od"):
            self.ensure(s)

    def ensure(self, s):
        if s not in self.semh:
            self.semh[s] = self.stack.enter_context(self.nc.semaphore(s))
            self.semv[s] = 0

    def emit(self, eng, deps, build, inc, inck=1):
        self.ensure(inc)
        for s, _v in deps:
            self.ensure(s)
        deps2 = []
        for s, v in deps:
            if v <= 0:
                continue
            if self.lastw[eng].get(s, 0) >= v:
                continue
            self.lastw[eng][s] = v
            deps2.append((s, v))
        self.semv[inc] += inck
        self.q[eng].append((deps2, build, inc, inck))
        return self.semv[inc]

    def val(self, sem):
        return self.semv[sem]

    def snap(self):
        return dict(self.semv)

    def bar(self, snapshot, skip=()):
        return [(s, v) for s, v in snapshot.items() if s not in skip]

    def wait_only(self, eng, deps):
        for s, _v in deps:
            self.ensure(s)
        deps2 = []
        for s, v in deps:
            if v <= 0 or self.lastw[eng].get(s, 0) >= v:
                continue
            self.lastw[eng][s] = v
            deps2.append((s, v))
        if deps2:
            self.q[eng].append((deps2, None, None, 0))

    def replay(self, eng_name, eng):
        for deps, build, incs, inck in self.q[eng_name]:
            for s, v in deps:
                eng.wait_ge(self.semh[s], v)
            if build is not None:
                build(eng).then_inc(self.semh[incs], inck)


def build_program():
    nc = bass.Bass("TRN2", target_bir_lowering=False)
    stack = contextlib.ExitStack()

    def din(name, shape, dtype=F32):
        return nc.dram_tensor(name, list(shape), dtype, kind="ExternalInput").ap()

    x_tok = din("x_tok", (TLOC, D))
    x_dsl = din("x_dsl", (DSL, L))
    xpwT = din("xpwT", (D, PRJ))
    routT = din("routT", (D, E))
    dtwT = din("dtwT", (RNK, DSL))
    dtb2 = din("dtb2", (128, 2))
    alog2 = din("alog2", (128, 2 * NST))
    dcol2 = din("dcol2", (128, 2))
    nwsl = din("nwsl", (128, 2))
    nbsl = din("nbsl", (128, 2))
    nwk = din("nwk", (128, 16))
    nbk = din("nbk", (128, 16))
    outwT = din("outwT", (D, D))
    w1q = din("w1q", (E, 11, 128, 2, 16, 128))
    w2q = din("w2q", (E, 4, 11, 128, 512))
    ident = din("ident", (128, 128))
    triuT = din("triuT", (128, 128))
    onesq = din("onesq", (128, 128))
    ceoff = din("ceoff", (128, E))
    tokid = din("tokid", (128, NTT), I32)
    zwidx = din("zwidx", (128, 16), I32)

    out_tok = nc.dram_tensor("out_tok", [TLOC, D], F32, kind="ExternalOutput").ap()

    projb = nc.dram_tensor("projb", [BROW, TLOC], F32).ap()
    pag = nc.dram_tensor("pag", [NCORES * BROW, TLOC], F32).ap()
    zb = nc.dram_tensor("zb", [DSL, L], F32).ap()
    zag = nc.dram_tensor("zag", [D * NCORES, FD], F32).ap()
    xnb = nc.dram_tensor("xnb", [TLOC, D], F32).ap()
    toksD = nc.dram_tensor("toksD", [NSLOT, 1], I32).ap()
    y2w = nc.dram_tensor("y2w", [NSLOT, D], F32).ap()
    flsh = nc.dram_tensor("flsh", [4, 128], F32).ap()

    r = Rec(nc, stack)
    TC, VC, SC, GC, OD = "tc", "vc", "sc", "gc", "od"

    def dma(eng, sem, out, in_, deps=()):
        return r.emit(eng, deps, lambda e, o=out, i=in_: e.dma_start(out=o, in_=i),
                      inc=sem, inck=16)

    def sbuf(name, shape, dtype=F32, scope=None):
        return (scope or stack).enter_context(nc.sbuf_tensor(name, list(shape), dtype))

    def psum(name, scope):
        return scope.enter_context(nc.psum_tensor(name, [128, 512], F32))

    c_id = sbuf("c_id", (128, 128))
    c_triu = sbuf("c_triu", (128, 128))
    c_ones = sbuf("c_ones", (128, 128))
    c_ceoff = sbuf("c_ceoff", (128, E))
    c_tokid = sbuf("c_tokid", (128, NTT), I32)
    c_zwidx = sbuf("c_zwidx", (128, 16), I32)
    s_xpw = sbuf("s_xpw", (128, 16 * PRJ))
    s_rout = sbuf("s_rout", (128, 16 * E))
    s_dtw = sbuf("s_dtw", (128, DSL))
    s_dtb = sbuf("s_dtb", (128, 2))
    s_alog = sbuf("s_alog", (128, 2 * NST))
    s_A = sbuf("s_A", (128, 2 * NST))
    s_dcol = sbuf("s_dcol", (128, 2))
    s_nwsl = sbuf("s_nwsl", (128, 2))
    s_nbsl = sbuf("s_nbsl", (128, 2))
    s_nwk = sbuf("s_nwk", (128, 16))
    s_nbk = sbuf("s_nbk", (128, 16))
    s_lg = sbuf("s_lg", (128, NTT * E))
    s_w12 = sbuf("s_w12", (128, 2 * NTT))
    s_msk = sbuf("s_msk", (128, NTT * E))
    s_m1 = sbuf("s_m1", (128, NTT * E))
    s_m2 = sbuf("s_m2", (128, NTT * E))
    s_sloti = sbuf("s_sloti", (128, 2 * NTT), I32)
    s_stoks = sbuf("s_stoks", (128, NSLOT // 128), I32)
    s_zero = sbuf("s_zero", (128, 128))
    s_col = sbuf("s_col", (128, 16))
    s_e8 = sbuf("s_e8", (128, 4 * E))

    dma("sync", "cst", c_id[:, :], ident)
    dma("sync", "cst", c_triu[:, :], triuT)
    dma("sync", "cst", c_ones[:, :], onesq)
    dma("sync", "cst", c_ceoff[:, :], ceoff)
    dma("sync", "cst", c_tokid[:, :], tokid)
    dma("sync", "cst", c_zwidx[:, :], zwidx)
    for k in range(16):
        dma("sync", "cst", s_xpw[:, k * PRJ:(k + 1) * PRJ], xpwT[k * 128:(k + 1) * 128, :])
        dma("sync", "cst", s_rout[:, k * E:(k + 1) * E], routT[k * 128:(k + 1) * 128, :])
    dma("sync", "cst", s_dtw[:, :], dtwT)
    dma("sync", "cst", s_dtb[:, :], dtb2)
    dma("sync", "cst", s_alog[:, :], alog2)
    dma("sync", "cst", s_dcol[:, :], dcol2)
    dma("sync", "cst", s_nwsl[:, :], nwsl)
    dma("sync", "cst", s_nbsl[:, :], nbsl)
    dma("sync", "cst", s_nwk[:, :], nwk)
    dma("sync", "cst", s_nbk[:, :], nbk)
    v_consts = r.val("cst")

    vs = r.emit("scalar", [("cst", v_consts)],
                lambda e: e.activation(s_A[:, :], s_alog[:, :], AF.Exp), inc=SC)
    vA = r.emit("vector", [(SC, vs)],
                lambda e: e.tensor_scalar_mul(s_A[:, :], s_A[:, :], -1.0), inc=VC)
    vz = r.emit("vector", [], lambda e: e.memset(s_zero[:, :], 0.0), inc=VC)
    for zi in range(16):
        dma("scalar", "tz", y2w[TRASH:TRASH + 128, zi * 128:(zi + 1) * 128],
            s_zero[:, :], deps=[(VC, vz)])
    for i in range(NSLOT // 128):
        dma("scalar", "tz", toksD[i * 128:(i + 1) * 128, :],
            s_zero[:, i:i + 1].bitcast(I32), deps=[(VC, vz)])
    v_toks_zero = r.val("tz")

    tok_scope = contextlib.ExitStack()
    s_strow = sbuf("s_strow", (4, TLOC), scope=tok_scope)
    A = sbuf("A", (128, 12544), scope=tok_scope)
    a_xt = A[:, 0:2048]
    a_sq = A[:, 2048:4096]
    a_xn = A[:, 4096:6144]
    a_x2 = A[:, 6144:8192]
    a_xnT = A[:, 8192:10240]
    a_x2T = A[:, 10240:12288]
    a_prj = A[:, 12288:12288 + 256]
    p_t1 = psum("p_t1", tok_scope)
    p_t2 = psum("p_t2", tok_scope)
    p_pr = psum("p_pr", tok_scope)
    p_lg = psum("p_lg", tok_scope)
    p_st = psum("p_st", tok_scope)
    p_rk = psum("p_rk", tok_scope)

    c0, c1, c2, c3 = s_col[:, 0:1], s_col[:, 1:2], s_col[:, 2:3], s_col[:, 3:4]
    mu1, rs1, mu2, rs2 = s_col[:, 4:5], s_col[:, 5:6], s_col[:, 6:7], s_col[:, 7:8]
    stc = s_col[:, 8:12]

    tc_e = sc_e = vc_e = gd_e = 0
    pt_free = [0, 0]
    pr_free = 0
    lg_free = 0
    prj_dma = 0
    v_xn_dma = []
    v_strow = 0
    v_logit = []

    def ln_stats(src, mu, rs, extra_v, extra_s):
        v = r.emit("vector", extra_v, lambda e, src=src: e.reduce_sum(c0, src, axis=AX.X),
                   inc=VC)
        s = r.emit("scalar", extra_s, lambda e, src=src: e.activation(
            a_sq, src, AF.Square), inc=SC)
        v = r.emit("vector", [(SC, s), (VC, v)],
                   lambda e: e.reduce_sum(c1, a_sq, axis=AX.X), inc=VC)
        v = r.emit("vector", [(VC, v)],
                   lambda e, mu=mu: e.tensor_scalar_mul(mu, c0, 1.0 / D), inc=VC)
        v = r.emit("vector", [(VC, v)],
                   lambda e: e.tensor_scalar_mul(c2, c1, 1.0 / D), inc=VC)
        v = r.emit("vector", [(VC, v)],
                   lambda e, mu=mu: e.tensor_mul(c3, mu, mu), inc=VC)
        v = r.emit("vector", [(VC, v)],
                   lambda e: e.tensor_sub(c3, c2, c3), inc=VC)
        v = r.emit("vector", [(VC, v)],
                   lambda e: e.tensor_scalar_add(c3, c3, EPS), inc=VC)
        s = r.emit("scalar", [(VC, v)],
                   lambda e: e.activation(c3, c3, AF.Sqrt), inc=SC)
        v = r.emit("vector", [(SC, s)],
                   lambda e, rs=rs: e.reciprocal(rs, c3), inc=VC)
        return v

    for tt in range(NTT):
        vxd = dma("sync", "xl", a_xt, x_tok[tt * 128:(tt + 1) * 128, :],
                  deps=[(VC, vc_e), (SC, sc_e)])
        v = ln_stats(a_xt, mu1, rs1, [("xl", vxd)], [("xl", vxd), (VC, vc_e)])
        v_xn = r.emit("vector", [(VC, v), (TC, tc_e), ("xnb", gd_e)],
                      lambda e: e.tensor_scalar(a_xn, a_xt, mu1, rs1,
                                                ALU.subtract, ALU.mult), inc=VC)
        v = ln_stats(a_xn, mu2, rs2, [(VC, v_xn)], [(VC, v_xn)])
        v_x2 = r.emit("vector", [(VC, v), (TC, tc_e)],
                      lambda e: e.tensor_scalar(a_x2, a_xn, mu2, rs2,
                                                ALU.subtract, ALU.mult), inc=VC)
        v_xn_dma.append(dma("gpsimd", "xnb", xnb[tt * 128:(tt + 1) * 128, :], a_xn,
                            deps=[(VC, v_xn)]))
        va = r.emit("vector", [(VC, v_x2)],
                    lambda e: e.tensor_copy(s_col[:, 8:9], rs1), inc=VC)
        va = r.emit("vector", [(VC, va)],
                    lambda e: e.tensor_mul(s_col[:, 9:10], mu1, rs1), inc=VC)
        va = r.emit("vector", [(VC, va)],
                    lambda e: e.tensor_copy(s_col[:, 10:11], rs2), inc=VC)
        va = r.emit("vector", [(VC, va)],
                    lambda e: e.tensor_mul(s_col[:, 11:12], mu2, rs2), inc=VC)
        vt = r.emit("tensor", [(VC, va), ("cst", v_consts), (SC, v_strow)],
                    lambda e: e.transpose(p_st[0:4, 0:128], stc, c_id[:, :]), inc=TC)
        v_strow = r.emit("scalar", [(TC, vt)],
                         lambda e, tt=tt: e.activation(
                             s_strow[0:4, tt * 128:(tt + 1) * 128],
                             p_st[0:4, 0:128], AF.Copy), inc=SC)
        v_evac = 0
        for src, dst, scaled, vdep in ((a_xn, a_xnT, False, v_xn),
                                       (a_x2, a_x2T, True, v_x2)):
            for k in range(16):
                pi = k % 2
                pp = p_t1 if pi == 0 else p_t2
                vt = r.emit("tensor", [(VC, vdep), (SC, pt_free[pi])],
                            lambda e, pp=pp, src=src, k=k: e.transpose(
                                pp[:, 0:128], src[:, k * 128:(k + 1) * 128],
                                c_id[:, :]), inc=TC)
                v_evac = r.emit(
                    "scalar", [(TC, vt), ("cst", v_consts)],
                    (lambda e, pp=pp, dst=dst, k=k: e.activation(
                        dst[:, k * 128:(k + 1) * 128], pp[:, 0:128], AF.Identity,
                        bias=s_nbk[:, k:k + 1], scale=s_nwk[:, k:k + 1]))
                    if scaled else
                    (lambda e, pp=pp, dst=dst, k=k: e.activation(
                        dst[:, k * 128:(k + 1) * 128], pp[:, 0:128], AF.Copy)),
                    inc=SC)
                pt_free[pi] = v_evac
        for pt in range(2):
            for k in range(16):
                vt = r.emit("tensor",
                            [(SC, v_evac)] + ([(SC, pr_free)] if k == 0 else []),
                            lambda e, pt=pt, k=k: e.matmul(
                                p_pr[:, 0:128],
                                s_xpw[:, k * PRJ + pt * 128:k * PRJ + (pt + 1) * 128],
                                a_x2T[:, k * 128:(k + 1) * 128],
                                start=(k == 0), stop=(k == 15)), inc=TC)
            pr_free = r.emit("scalar", [(TC, vt), ("prj", prj_dma)],
                             lambda e, pt=pt: e.activation(
                                 a_prj[:, 0:128], p_pr[:, 0:128], AF.Copy), inc=SC)
            prj_dma = dma("gpsimd", "prj",
                          projb[pt * 128:(pt + 1) * 128, tt * 128:(tt + 1) * 128],
                          a_prj[:, 0:128], deps=[(SC, pr_free)])
        for k in range(16):
            vt = r.emit("tensor",
                        [(SC, v_evac)] + ([(SC, lg_free)] if k == 0 else []),
                        lambda e, k=k: e.matmul(
                            p_lg[:, 0:E], a_xnT[:, k * 128:(k + 1) * 128],
                            s_rout[:, k * E:(k + 1) * E],
                            start=(k == 0), stop=(k == 15)), inc=TC)
        lg_free = r.emit("scalar", [(TC, vt)],
                         lambda e, tt=tt: e.activation(
                             s_lg[:, tt * E:(tt + 1) * E], p_lg[:, 0:E], AF.Copy),
                         inc=SC)
        v_logit.append(lg_free)
        tc_e, sc_e, vc_e, gd_e = r.val(TC), r.val(SC), r.val(VC), r.val("xnb")

    e0, e1t, e2t, e3t = (s_e8[:, 0:E], s_e8[:, E:2 * E], s_e8[:, 2 * E:3 * E],
                         s_e8[:, 3 * E:4 * E])

    def cumsum8(src, t1, t2, dep):
        v = r.emit("vector", [(VC, dep)], lambda e: e.tensor_copy(t1, src), inc=VC)
        v = r.emit("vector", [(VC, v)],
                   lambda e: e.tensor_add(t1[:, 1:8], src[:, 1:8], src[:, 0:7]), inc=VC)
        v = r.emit("vector", [(VC, v)], lambda e: e.tensor_copy(t2, t1), inc=VC)
        v = r.emit("vector", [(VC, v)],
                   lambda e: e.tensor_add(t2[:, 2:8], t1[:, 2:8], t1[:, 0:6]), inc=VC)
        v = r.emit("vector", [(VC, v)], lambda e: e.tensor_copy(t1, t2), inc=VC)
        v = r.emit("vector", [(VC, v)],
                   lambda e: e.tensor_add(t1[:, 4:8], t2[:, 4:8], t2[:, 0:4]), inc=VC)
        return t1, v

    for tt in range(NTT):
        lg = s_lg[:, tt * E:(tt + 1) * E]
        m1 = s_m1[:, tt * E:(tt + 1) * E]
        m2 = s_m2[:, tt * E:(tt + 1) * E]
        v = r.emit("vector", [(SC, v_logit[tt])],
                   lambda e, lg=lg: e.reduce_max(c0, lg, axis=AX.X), inc=VC)
        v = r.emit("vector", [(VC, v)],
                   lambda e, lg=lg: e.tensor_single_scalar(e0, lg, c0, ALU.is_equal),
                   inc=VC)
        cs, v = cumsum8(e0, e1t, e2t, v)
        v = r.emit("vector", [(VC, v)],
                   lambda e, cs=cs: e.tensor_single_scalar(e2t, cs, 1.0, ALU.is_equal),
                   inc=VC)
        v = r.emit("vector", [(VC, v)], lambda e, m1=m1: e.tensor_mul(m1, e0, e2t),
                   inc=VC)
        v = r.emit("vector", [(VC, v)],
                   lambda e, m1=m1: e.tensor_scalar_mul(e0, m1, 1e30), inc=VC)
        v = r.emit("vector", [(VC, v)], lambda e, lg=lg: e.tensor_sub(e0, lg, e0),
                   inc=VC)
        v = r.emit("vector", [(VC, v)], lambda e: e.reduce_max(c1, e0, axis=AX.X),
                   inc=VC)
        v = r.emit("vector", [(VC, v)],
                   lambda e: e.tensor_single_scalar(e1t, e0, c1, ALU.is_equal), inc=VC)
        cs2, v = cumsum8(e1t, e2t, e3t, v)
        v = r.emit("vector", [(VC, v)],
                   lambda e, cs2=cs2: e.tensor_single_scalar(e3t, cs2, 1.0,
                                                             ALU.is_equal), inc=VC)
        v = r.emit("vector", [(VC, v)], lambda e, m2=m2: e.tensor_mul(m2, e1t, e3t),
                   inc=VC)
        v = r.emit("vector", [(VC, v)], lambda e: e.tensor_sub(c2, c1, c0), inc=VC)
        s = r.emit("scalar", [(VC, v)], lambda e: e.activation(c3, c2, AF.Exp), inc=SC)
        v = r.emit("vector", [(SC, s)], lambda e: e.tensor_scalar_add(c2, c3, 1.0),
                   inc=VC)
        v = r.emit("vector", [(VC, v)],
                   lambda e, tt=tt: e.reciprocal(s_w12[:, 2 * tt:2 * tt + 1], c2),
                   inc=VC)
        v = r.emit("vector", [(VC, v)],
                   lambda e, tt=tt: e.tensor_mul(s_w12[:, 2 * tt + 1:2 * tt + 2], c3,
                                                 s_w12[:, 2 * tt:2 * tt + 1]), inc=VC)
        v = r.emit("vector", [(VC, v)],
                   lambda e, m1=m1, m2=m2, tt=tt: e.tensor_add(
                       s_msk[:, tt * E:(tt + 1) * E], m1, m2), inc=VC)
    v_masks = r.val(VC)

    rk_free = 0
    v_slots = 0
    for tt in range(NTT):
        for j in range(tt + 1):
            lhs = c_ones if j < tt else c_triu
            deps = [(VC, v_masks), ("cst", v_consts)]
            if j == 0:
                deps.append((SC, rk_free))
            vt = r.emit("tensor", deps,
                        lambda e, j=j, tt=tt, lhs=lhs: e.matmul(
                            p_rk[:, 0:E], lhs[:, :], s_msk[:, j * E:(j + 1) * E],
                            start=(j == 0), stop=(j == tt)), inc=TC)
        rk_free = r.emit("scalar", [(TC, vt), (VC, v_masks)],
                         lambda e, tt=tt: e.activation(
                             s_lg[:, tt * E:(tt + 1) * E], p_rk[:, 0:E], AF.Copy),
                         inc=SC)
        rk = s_lg[:, tt * E:(tt + 1) * E]
        v = r.emit("vector", [(SC, rk_free), ("cst", v_consts)],
                   lambda e, rk=rk: e.tensor_add(e0, rk, c_ceoff[:, :]), inc=VC)
        v = r.emit("vector", [(VC, v)],
                   lambda e, rk=rk: e.tensor_single_scalar(
                       e1t, rk, float(CAP) + 0.5, ALU.is_gt), inc=VC)
        v = r.emit("vector", [(VC, v)],
                   lambda e: e.tensor_scalar(e2t, e0, -1.0, float(TRASH),
                                             ALU.mult, ALU.add), inc=VC)
        v = r.emit("vector", [(VC, v)], lambda e: e.tensor_mul(e2t, e2t, e1t), inc=VC)
        v = r.emit("vector", [(VC, v)], lambda e: e.tensor_add(e0, e0, e2t), inc=VC)
        v = r.emit("vector", [(VC, v)],
                   lambda e, tt=tt: e.tensor_mul(e2t, s_m1[:, tt * E:(tt + 1) * E], e0),
                   inc=VC)
        v = r.emit("vector", [(VC, v)],
                   lambda e: e.reduce_sum(c0, e2t, axis=AX.X), inc=VC)
        v = r.emit("vector", [(VC, v)],
                   lambda e, tt=tt: e.tensor_mul(e2t, s_m2[:, tt * E:(tt + 1) * E], e0),
                   inc=VC)
        v = r.emit("vector", [(VC, v)],
                   lambda e: e.reduce_sum(c1, e2t, axis=AX.X), inc=VC)
        v = r.emit("vector", [(VC, v)],
                   lambda e, tt=tt: e.tensor_copy(s_sloti[:, 2 * tt:2 * tt + 1], c0),
                   inc=VC)
        v_slots = r.emit("vector", [(VC, v)],
                         lambda e, tt=tt: e.tensor_copy(
                             s_sloti[:, 2 * tt + 1:2 * tt + 2], c1), inc=VC)

    for tt in range(NTT):
        for si in range(2):
            r.emit("gpsimd", [(VC, v_slots), ("tz", v_toks_zero), ("cst", v_consts)],
                   lambda e, tt=tt, si=si: e.indirect_dma_start(
                       out=toksD, out_offset=bass.IndirectOffsetOnAxis(
                           ap=s_sloti[:, 2 * tt + si:2 * tt + si + 1], axis=0),
                       in_=c_tokid[:, tt:tt + 1], in_offset=None),
                   inc="sca", inck=16)
    v_scat = dma("gpsimd", "sca", flsh[0:1, 0:1], toksD[0:1, 0:1].bitcast(F32),
                 deps=[("sca", r.val("sca"))])
    for cc in range(NSLOT // 128):
        vtk = dma("sync", "stk", s_stoks[:, cc:cc + 1],
                  toksD[cc * 128:(cc + 1) * 128, :], deps=[("sca", v_scat)])

    dma("gpsimd", "prj", projb[256:260, :], s_strow[:, :], deps=[(SC, v_strow)])
    v_projb = r.val("prj")

    r.wait_only("gpsimd", [("prj", v_projb)])
    r.emit("gpsimd", [],
           lambda e: e.collective_compute(
               "AllGather", ALU.bypass, replica_groups=[list(range(NCORES))],
               ins=[projb.opt()], outs=[pag.opt()]), inc=GC)
    v_ag = r.val(GC)
    tok_scope.close()
    tok_snap = r.snap()
    barT = r.bar(tok_snap, skip=("od",))

    moe_scope = contextlib.ExitStack()
    b_xgtm = sbuf("b_xgtm", (128, 2048), scope=moe_scope)
    b_xgfm = sbuf("b_xgfm", (128, 16 * CAP), scope=moe_scope)
    s_yact = sbuf("s_yact", (128, 11 * CAP), scope=moe_scope)
    s_w1 = sbuf("s_w1", (128, 2 * 2048), scope=moe_scope)
    s_w2 = sbuf("s_w2", (128, 2 * 512), scope=moe_scope)
    s_sg = sbuf("s_sg", (128, CAP), scope=moe_scope)
    s_y2s = sbuf("s_y2s", (128, 2 * 512), scope=moe_scope)

    scan_scope = contextlib.ExitStack()
    s_dr = sbuf("s_dr", (128, FD), scope=scan_scope)
    s_stb = sbuf("s_stb", (128, 4 * FD), scope=scan_scope)
    s_xc = sbuf("s_xc", (128, FD), scope=scan_scope)
    s_t0 = sbuf("s_t0", (128, FD), scope=scan_scope)
    s_dl = sbuf("s_dl", (128, 2 * FD), scope=scan_scope)
    s_dx = sbuf("s_dx", (128, 2 * FD), scope=scan_scope)
    s_ysb = sbuf("s_ysb", (128, 2 * FD), scope=scan_scope)
    s_dA = sbuf("s_dA", (128, 2 * FD), scope=scan_scope)
    s_u = sbuf("s_u", (128, 2 * FD), scope=scan_scope)
    s_h = sbuf("s_h", (128, 3 * FD), scope=scan_scope)
    s_tmp = sbuf("s_tmp", (128, 3 * FD), scope=scan_scope)
    s_bbc = sbuf("s_bbc", (128, 2 * FD), scope=scan_scope)
    s_cbc = sbuf("s_cbc", (128, 2 * FD), scope=scan_scope)
    s_state = sbuf("s_state", (128, 2 * NST), scope=scan_scope)

    p_gu = psum("p_gu", moe_scope)
    p_m2 = [psum(f"p_m2_{m}", moe_scope) for m in range(2)]
    p_dt = psum("p_dt", scan_scope)
    p_y = [scan_scope.enter_context(nc.psum_tensor(f"p_y{g}", [128, FD], F32))
           for g in range(2)]

    w1_free = [0, 0]
    w2_free = [0, 0]
    w1s = ["w1a", "w1b"]
    w2s = ["w2a", "w2b"]
    y2s = ["y2a", "y2b"]
    y2s_free = [0, 0]
    pm_free = [0, 0]
    v_xn_all = v_xn_dma[-1]

    def moe_units():
        pgu_free = 0
        pgu_free_sc = 0
        v_yact = 0
        v_tr = 0
        xg_free = 0
        it_w1 = 0
        for ee in range(E):
            for mc in range(3):
                vg = r.emit(
                    "gpsimd", [("stk", vtk), ("xnb", v_xn_all), (TC, xg_free)],
                    lambda e, ee=ee, mc=mc: e.indirect_dma_start(
                        out=b_xgtm[:, :], out_offset=None,
                        in_=xnb, in_offset=bass.IndirectOffsetOnAxis(
                            ap=s_stoks[:, ee * 3 + mc:ee * 3 + mc + 1], axis=0)),
                    inc="mg", inck=16)
                vg = dma("gpsimd", "mg", flsh[1:2, :], b_xgtm[0:1, 0:128],
                         deps=[("mg", vg)])
                for k in range(16):
                    deps = [("mg", vg), ("cst", v_consts), (SC, v_tr)]
                    if k == 0:
                        deps += [(VC, pgu_free), (SC, pgu_free_sc)]
                    vt = r.emit("tensor", deps,
                                lambda e, k=k: e.transpose(
                                    p_gu[:, 0:128],
                                    b_xgtm[:, k * 128:(k + 1) * 128],
                                    c_id[:, :]), inc=TC)
                    v_tr = r.emit("scalar", [(TC, vt)],
                                  lambda e, k=k, mc=mc: e.activation(
                                      b_xgfm[:, k * CAP + mc * 128:
                                             k * CAP + (mc + 1) * 128],
                                      p_gu[:, 0:128], AF.Copy), inc=SC)
                xg_free = r.val(TC)
                pgu_free_sc = v_tr
                yield None
            for iu in range(11):
                for j in range(2):
                    bufi = it_w1 % 2
                    it_w1 += 1
                    vw = dma("scalar", w1s[bufi],
                             s_w1[:, bufi * 2048:(bufi + 1) * 2048],
                             w1q[ee, iu, :, j].rearrange("dd k ii -> dd (k ii)"),
                             deps=[(TC, w1_free[bufi])] +
                             (barT if (ee == 0 and iu == 0) else []))
                    for k in range(16):
                        deps = [(w1s[bufi], vw), (SC, v_tr)]
                        if k == 0:
                            deps += [(SC, pgu_free_sc)] if j == 0 else []
                        vt = r.emit("tensor", deps,
                                    lambda e, k=k, bufi=bufi: e.matmul(
                                        p_gu[:, 0:CAP],
                                        s_w1[:, bufi * 2048 + k * 128:
                                             bufi * 2048 + (k + 1) * 128],
                                        b_xgfm[:, k * CAP:(k + 1) * CAP],
                                        start=(k == 0), stop=(k == 15)), inc=TC)
                    w1_free[bufi] = vt
                    if j == 0:
                        pgu_free_sc = r.emit(
                            "scalar", [(TC, vt), (VC, pgu_free)],
                            lambda e: e.activation(s_sg[:, :], p_gu[:, 0:CAP],
                                                   AF.Silu), inc=SC)
                    else:
                        v_yact = r.emit("vector", [(TC, vt), (SC, pgu_free_sc)],
                                        lambda e, iu=iu: e.tensor_mul(
                                            s_yact[:, iu * CAP:(iu + 1) * CAP],
                                            s_sg[:, :], p_gu[:, 0:CAP]), inc=VC)
                        pgu_free = v_yact
                    yield None
            for dc in range(4):
                for ph, ms in ((0, (0, 1)), (1, (2,))):
                    for k in range(11):
                        bufi = k % 2
                        vw2 = dma("scalar", w2s[bufi],
                                  s_w2[:, bufi * 512:(bufi + 1) * 512],
                                  w2q[ee, dc, k], deps=[(TC, w2_free[bufi])])
                        for mi, m in enumerate(ms):
                            deps = [(w2s[bufi], vw2), (VC, v_yact)]
                            if k == 0:
                                deps.append((SC, pm_free[mi]))
                            vt = r.emit("tensor", deps,
                                        lambda e, m=m, k=k, bufi=bufi, mi=mi:
                                        e.matmul(
                                            p_m2[mi][:, :],
                                            s_yact[:, k * CAP + m * 128:
                                                   k * CAP + (m + 1) * 128],
                                            s_w2[:, bufi * 512:(bufi + 1) * 512],
                                            start=(k == 0), stop=(k == 10)),
                                        inc=TC)
                        w2_free[bufi] = vt
                    for mi, m in enumerate(ms):
                        sbi = m % 2
                        pm_free[mi] = r.emit(
                            "scalar", [(TC, w2_free[(10) % 2]),
                                       (y2s[sbi], y2s_free[sbi])],
                            lambda e, mi=mi, sbi=sbi: e.activation(
                                s_y2s[:, sbi * 512:(sbi + 1) * 512],
                                p_m2[mi][:, :], AF.Copy), inc=SC)
                        y2s_free[sbi] = dma(
                            "scalar", y2s[sbi],
                            y2w[ee * CAP + m * 128:ee * CAP + (m + 1) * 128,
                                dc * 512:(dc + 1) * 512],
                            s_y2s[:, sbi * 512:(sbi + 1) * 512],
                            deps=[(SC, pm_free[mi])])
                    yield None

    moe_gen = moe_units()

    SL = {}
    for nm, t in (("dl", s_dl), ("dx", s_dx), ("ysb", s_ysb)):
        SL[nm] = [t[:, 0:FD], t[:, FD:2 * FD]]
    RG = {}
    for nm, t, nr in (("dA", s_dA, 2), ("u", s_u, 2), ("h", s_h, 3),
                      ("tmp", s_tmp, 3), ("bbc", s_bbc, 2), ("cbc", s_cbc, 2)):
        RG[nm] = [t[:, i * FD:(i + 1) * FD] for i in range(nr)]
    stbs = [s_stb[:, j * FD:(j + 1) * FD] for j in range(4)]

    bbc_used = [0, 0]
    cbc_used = [0, 0]
    dA_used = [0, 0]
    u_used = [0, 0]
    h_free_sc = [0, 0, 0]
    h_pool = [0, 0, 0]
    tmp_used = [0, 0, 0]
    y_evac = [0, 0]
    st_sc = [[0] * NST, [0] * NST]
    v_zb = [0, 0]
    zbs = ["z0", "z1"]
    bsem = ["b0", "b1"]
    csem = ["ca", "cb"]
    stb_free = 0
    dr_free = 0
    xc_free = 0
    vy_prev = [0, 0]
    pdt_free = 0
    t0_sc = 0
    it = 0

    for c in range(NCH):
        blk = c * BROW
        first = barT if c == 0 else []
        v_stb = 0
        for j in range(4):
            v_stb = dma("sync", "stb", stbs[j],
                        pag[blk + 256 + j:blk + 257 + j, :].to_broadcast([128, FD]),
                        deps=first + [(GC, v_ag), (VC, stb_free)])
        v_dr = dma("sync", "dr", s_dr[:, :], pag[blk:blk + 128, :],
                   deps=first + [(GC, v_ag), (TC, dr_free)])
        vdl = [0, 0]
        for g in range(2):
            vxc = dma("sync", "xc", s_xc[:, :],
                      x_dsl[g * 128:(g + 1) * 128, c * FD:(c + 1) * FD],
                      deps=first + [(VC, xc_free)])
            v = r.emit("vector", [("xc", vxc), ("stb", v_stb), (SC, vy_prev[g]),
                                  (SC, t0_sc)],
                       lambda e: e.tensor_mul(s_t0[:, :], s_xc[:, :], stbs[0]), inc=VC)
            xc_free = v
            v_xno = r.emit("vector", [(VC, v)],
                           lambda e: e.tensor_sub(s_t0[:, :], s_t0[:, :], stbs[1]),
                           inc=VC)
            vy = r.emit("scalar", [(VC, v_xno), ("cst", v_consts),
                                   (zbs[g], v_zb[g])],
                        lambda e, g=g: e.activation(SL["ysb"][g], s_t0[:, :], AF.Copy,
                                                    scale=s_dcol[:, g:g + 1]), inc=SC)
            vy_prev[g] = vy
            v = r.emit("vector", [(VC, v_xno)],
                       lambda e, g=g: e.tensor_mul(SL["dx"][g], s_t0[:, :], stbs[2]),
                       inc=VC)
            v = r.emit("vector", [(VC, v)],
                       lambda e, g=g: e.tensor_sub(SL["dx"][g], SL["dx"][g], stbs[3]),
                       inc=VC)
            vs2 = r.emit("scalar", [(VC, v), ("cst", v_consts)],
                         lambda e, g=g: e.activation(SL["dx"][g], SL["dx"][g],
                                                     AF.Identity,
                                                     bias=s_nbsl[:, g:g + 1],
                                                     scale=s_nwsl[:, g:g + 1]), inc=SC)
            va = 0
            for f in range(2):
                vt = r.emit("tensor",
                            [("dr", v_dr), ("cst", v_consts),
                             (VC, va if f == 1 else pdt_free)],
                            lambda e, g=g, f=f: e.matmul(
                                p_dt[:, :], s_dtw[:, g * 128:(g + 1) * 128],
                                s_dr[:, f * 512:(f + 1) * 512],
                                start=True, stop=True), inc=TC)
                dr_free = vt
                va = r.emit("vector", [(TC, vt), ("cst", v_consts), (SC, vy)],
                            lambda e, f=f, g=g: e.tensor_scalar_add(
                                s_t0[:, f * 512:(f + 1) * 512], p_dt[:, :],
                                s_dtb[:, g:g + 1]), inc=VC)
            pdt_free = va
            v = r.emit("vector", [(VC, va)],
                       lambda e: e.tensor_scalar_mul(s_xc[:, :], s_t0[:, :], -1.0),
                       inc=VC)
            v = r.emit("vector", [(VC, v)],
                       lambda e: e.tensor_tensor(s_xc[:, :], s_xc[:, :], s_t0[:, :],
                                                 ALU.min), inc=VC)
            vs = r.emit("scalar", [(VC, v)],
                        lambda e: e.activation(s_xc[:, :], s_xc[:, :], AF.Exp), inc=SC)
            vs = r.emit("scalar", [(SC, vs)],
                        lambda e: e.activation(s_xc[:, :], s_xc[:, :], AF.Ln, bias=1.0),
                        inc=SC)
            vs = r.emit("scalar", [(SC, vs)],
                        lambda e, g=g: e.activation(SL["dl"][g], s_t0[:, :], AF.Relu),
                        inc=SC)
            t0_sc = vs
            v = r.emit("vector", [(SC, vs)],
                       lambda e, g=g: e.tensor_add(SL["dl"][g], SL["dl"][g],
                                                   s_xc[:, :]), inc=VC)
            xc_free = v
            vdl[g] = v
            r.emit("vector", [(VC, v), (SC, vs2)],
                   lambda e, g=g: e.tensor_mul(SL["dx"][g], SL["dx"][g], SL["dl"][g]),
                   inc=VC)
        v_prep = r.val(VC)
        stb_free = v_prep

        for n in range(NST):
            bi = (c * NST + n) % 2
            ci = (c * NST + n) % 2
            vb = dma("sync", bsem[bi], RG["bbc"][bi],
                     pag[blk + 128 + n:blk + 129 + n, :].to_broadcast([128, FD]),
                     deps=[(VC, bbc_used[bi]), (GC, v_ag)])
            vc1 = dma("sync", csem[ci], RG["cbc"][ci][:, 0:PER],
                      pag[192 + n:193 + n, 0:PER].to_broadcast([128, PER]),
                      deps=[(GC, cbc_used[ci]), (GC, v_ag)])
            vc1 = dma("sync", csem[ci], RG["cbc"][ci][:, PER:FD],
                      pag[192 + n:193 + n, 0:PER].to_broadcast([128, PER]),
                      deps=[(GC, v_ag)])
            for g in range(2):
                di = it % 2
                vsA = r.emit("scalar",
                             [(VC, max(vdl[g], dA_used[di])), (VC, vA)],
                             lambda e, g=g, n=n, di=di: e.activation(
                                 RG["dA"][di], SL["dl"][g], AF.Exp,
                                 scale=s_A[:, g * NST + n:g * NST + n + 1]), inc=SC)
                vu = r.emit("vector", [(bsem[bi], vb),
                                       (VC, max(u_used[di], v_prep))],
                            lambda e, g=g, di=di, bi=bi: e.tensor_mul(
                                RG["u"][di], SL["dx"][g], RG["bbc"][bi]), inc=VC)
                bbc_used[bi] = vu
                hi = it % 3
                init = 0.0 if c == 0 else s_state[:, g * NST + n:g * NST + n + 1]
                vh = r.emit("vector",
                            [(SC, vsA), (VC, vu), (SC, h_free_sc[hi]),
                             (GC, h_pool[hi])] +
                            ([(SC, st_sc[g][n])] if c > 0 else []),
                            lambda e, hi=hi, di=di, init=init: e.tensor_tensor_scan(
                                RG["h"][hi], RG["dA"][di], RG["u"][di], init,
                                ALU.mult, ALU.add), inc=VC)
                dA_used[di] = vh
                u_used[di] = vh
                st_sc[g][n] = r.emit("scalar", [(VC, vh)],
                                     lambda e, hi=hi, g=g, n=n: e.activation(
                                         s_state[:, g * NST + n:g * NST + n + 1],
                                         RG["h"][hi][:, FD - 1:FD], AF.Copy), inc=SC)
                h_free_sc[hi] = st_sc[g][n]
                ti = it % 3
                vtm = r.emit("gpsimd", [(csem[ci], vc1), (TC, tmp_used[ti]),
                                        (VC, vh)],
                             lambda e, hi=hi, ti=ti, ci=ci: e.tensor_mul(
                                 RG["tmp"][ti], RG["h"][hi], RG["cbc"][ci]), inc=GC)
                cbc_used[ci] = vtm
                h_pool[hi] = vtm
                if n == 0:
                    vt = r.emit("tensor", [(SC, vy_prev[g]), ("cst", v_consts)],
                                lambda e, g=g: e.matmul(
                                    p_y[g][:, 0:512], c_id[:, :],
                                    SL["ysb"][g][:, 0:512],
                                    start=True, stop=False), inc=TC)
                    vt = r.emit("tensor", [],
                                lambda e, g=g: e.matmul(
                                    p_y[g][:, 512:FD], c_id[:, :],
                                    SL["ysb"][g][:, 512:FD],
                                    start=True, stop=False), inc=TC)
                for half in range(2):
                    vt = r.emit("tensor", [(GC, vtm)],
                                lambda e, g=g, ti=ti, half=half, n=n: e.matmul(
                                    p_y[g][:, half * 512:(half + 1) * 512],
                                    c_id[:, :],
                                    RG["tmp"][ti][:, half * 512:(half + 1) * 512],
                                    start=False, stop=(n == NST - 1)), inc=TC)
                tmp_used[ti] = vt
                it += 1
                if it % 4 == 0:
                    next(moe_gen, None)
        for g in range(2):
            ve = r.emit("scalar", [(TC, tmp_used[(it - 1) % 3]),
                                   (zbs[g], v_zb[g])],
                        lambda e, g=g: e.activation(SL["ysb"][g], p_y[g][:, :],
                                                    AF.Copy), inc=SC)
            y_evac[g] = ve
            v_zb[g] = dma("sync", zbs[g],
                          zb[g * 128:(g + 1) * 128, c * FD:(c + 1) * FD],
                          SL["ysb"][g], deps=[(SC, ve)])
    for _ in moe_gen:
        pass
    scan_scope.close()
    moe_scope.close()

    r.wait_only("gpsimd", [(zbs[0], v_zb[0]), (zbs[1], v_zb[1])])
    r.emit("gpsimd", [],
           lambda e: e.collective_compute(
               "AllGather", ALU.bypass, replica_groups=[list(range(NCORES))],
               ins=[zb.opt()], outs=[zag.opt()]), inc=GC)
    v_zag = r.val(GC)
    scan_snap = r.snap()
    barF = r.bar(scan_snap, skip=("od",))

    fin_scope = contextlib.ExitStack()
    b_acc = sbuf("b_acc", (128, NTT * 2048), scope=fin_scope)
    b_g1 = sbuf("b_g1", (128, 2048), scope=fin_scope)
    b_g2 = sbuf("b_g2", (128, 2048), scope=fin_scope)
    b_xr = sbuf("b_xr", (128, 2048), scope=fin_scope)
    zA = sbuf("zA", (128, 16 * FD), scope=fin_scope)
    s_ow = sbuf("s_ow", (128, 2 * 512), scope=fin_scope)
    p_att = [psum(f"p_att{i}", fin_scope) for i in range(8)]

    v_zg = []
    for k in range(16):
        v_zg.append(r.emit(
            "gpsimd", [(GC, v_zag), ("cst", v_consts)] + (barF if k == 0 else []),
            lambda e, k=k: e.indirect_dma_start(
                out=zA[:, k * FD:(k + 1) * FD], out_offset=None, in_=zag,
                in_offset=bass.IndirectOffsetOnAxis(ap=c_zwidx[:, k:k + 1], axis=0)),
            inc="zg", inck=16))
    v_zgf = dma("gpsimd", "zg", flsh[2:3, :], zA[0:1, 0:128], deps=[("zg", v_zg[15])])
    v_zg = [v_zgf] * 16

    v_acc = []
    for tt in range(NTT):
        vg1 = r.emit("gpsimd",
                     [(y2s[0], y2s_free[0]), (y2s[1], y2s_free[1]),
                      (VC, v_slots)] +
                     ([(VC, v_acc[-1])] if v_acc else barF),
                     lambda e, tt=tt: e.indirect_dma_start(
                         out=b_g1[:, :], out_offset=None, in_=y2w,
                         in_offset=bass.IndirectOffsetOnAxis(
                             ap=s_sloti[:, 2 * tt:2 * tt + 1], axis=0)),
                     inc="fg", inck=16)
        vg2 = r.emit("gpsimd", [],
                     lambda e, tt=tt: e.indirect_dma_start(
                         out=b_g2[:, :], out_offset=None, in_=y2w,
                         in_offset=bass.IndirectOffsetOnAxis(
                             ap=s_sloti[:, 2 * tt + 1:2 * tt + 2], axis=0)),
                     inc="fg", inck=16)
        vg2 = dma("gpsimd", "fg", flsh[3:4, :], b_g2[0:1, 0:128], deps=[("fg", vg2)])
        vg1 = vg2
        vxr = dma("sync", "xr", b_xr[:, :], x_tok[tt * 128:(tt + 1) * 128, :],
                  deps=([(VC, v_acc[-1])] if v_acc else barF))
        acc = b_acc[:, tt * 2048:(tt + 1) * 2048]
        v = r.emit("vector", [("fg", vg1)] + (barF if tt == 0 else []),
                   lambda e, acc=acc, tt=tt: e.tensor_scalar(
                       acc, b_g1[:, :], s_w12[:, 2 * tt:2 * tt + 1], None, ALU.mult),
                   inc=VC)
        v = r.emit("vector", [("fg", vg2), (VC, v)],
                   lambda e, acc=acc, tt=tt: e.scalar_tensor_tensor(
                       acc, b_g2[:, :], s_w12[:, 2 * tt + 1:2 * tt + 2], acc,
                       ALU.mult, ALU.add), inc=VC)
        v = r.emit("vector", [("xr", vxr), (VC, v)],
                   lambda e, acc=acc: e.tensor_add(acc, acc, b_xr[:, :]), inc=VC)
        v_acc.append(v)
    v_acc_all = r.val(VC)

    ow_free = [0, 0]
    ows = ["owa", "owb"]
    att_done = [0] * 8
    att_mm = [0] * 8
    for dc in range(4):
        for k in range(16):
            bufi = k % 2
            vw = dma("scalar", ows[bufi], s_ow[:, bufi * 512:(bufi + 1) * 512],
                     outwT[k * 128:(k + 1) * 128, dc * 512:(dc + 1) * 512],
                     deps=[(TC, ow_free[bufi])] + (barF if (dc == 0 and k < 2) else []))
            for tt in range(NTT):
                deps = [(ows[bufi], vw), ("zg", v_zg[k])]
                if k == 0:
                    deps.append((VC, att_done[tt]))
                att_mm[tt] = r.emit("tensor", deps,
                                    lambda e, k=k, tt=tt, bufi=bufi: e.matmul(
                                        p_att[tt][:, :],
                                        zA[:, k * FD + tt * 128:k * FD + (tt + 1) * 128],
                                        s_ow[:, bufi * 512:(bufi + 1) * 512],
                                        start=(k == 0), stop=(k == 15)), inc=TC)
            ow_free[bufi] = att_mm[-1]
        for tt in range(NTT):
            acc = b_acc[:, tt * 2048 + dc * 512:tt * 2048 + (dc + 1) * 512]
            v = r.emit("vector", [(TC, att_mm[tt]), (VC, v_acc_all)],
                       lambda e, acc=acc, tt=tt: e.tensor_add(acc, acc,
                                                              p_att[tt][:, :]), inc=VC)
            att_done[tt] = v
            if dc == 3:
                dma("sync", "od", out_tok[tt * 128:(tt + 1) * 128, :],
                    b_acc[:, tt * 2048:(tt + 1) * 2048], deps=[(VC, v)])
    r.wait_only("gpsimd", [("od", r.val("od"))])
    fin_scope.close()

    with nc.Block() as block:
        @block.sync
        def _(eng):
            r.replay("sync", eng)

        @block.tensor
        def _(eng):
            r.replay("tensor", eng)

        @block.vector
        def _(eng):
            r.replay("vector", eng)

        @block.scalar
        def _(eng):
            r.replay("scalar", eng)

        @block.gpsimd
        def _(eng):
            r.replay("gpsimd", eng)

    stack.close()
    return nc


def kernel(L0_attn_x, L0_attn_norm_w, L0_attn_norm_b, L0_attn_x_proj_w,
           L0_attn_dt_proj_w, L0_attn_dt_proj_b, L0_attn_A_log, L0_attn_D_param,
           L0_attn_out_proj_w, L0_ffn_router_w, L0_ffn_w1, L0_ffn_w2):
    x = np.ascontiguousarray(np.asarray(L0_attn_x, dtype=np.float32).reshape(L, D))
    xT = np.ascontiguousarray(x.T)
    xpwT_h = np.ascontiguousarray(np.asarray(L0_attn_x_proj_w, np.float32).T)
    routT_h = np.ascontiguousarray(np.asarray(L0_ffn_router_w, np.float32).T)
    dtw = np.asarray(L0_attn_dt_proj_w, np.float32)
    dtb = np.asarray(L0_attn_dt_proj_b, np.float32)
    alog = np.asarray(L0_attn_A_log, np.float32)
    dpar = np.asarray(L0_attn_D_param, np.float32)
    nw = np.asarray(L0_attn_norm_w, np.float32)
    nb = np.asarray(L0_attn_norm_b, np.float32)
    outwT_h = np.ascontiguousarray(np.asarray(L0_attn_out_proj_w, np.float32).T)
    w1 = np.asarray(L0_ffn_w1, np.float32)
    w2 = np.asarray(L0_ffn_w2, np.float32)

    w1r = w1.reshape(E, 2, 11, 128, 16, 128)
    w1q_h = np.ascontiguousarray(w1r.transpose(0, 2, 5, 1, 4, 3))
    w2r = w2.reshape(E, 4, 512, 11, 128)
    w2q_h = np.ascontiguousarray(w2r.transpose(0, 1, 3, 4, 2))

    ident_h = np.eye(128, dtype=np.float32)
    triu_h = np.triu(np.ones((128, 128), np.float32))
    ones_h = np.ones((128, 128), np.float32)
    ceoff_h = np.ascontiguousarray(np.broadcast_to(
        (np.arange(E, dtype=np.float32) * CAP - 1.0)[None, :], (128, E)))
    tokid_h = np.ascontiguousarray(np.arange(TLOC, dtype=np.int32).reshape(NTT, 128).T)
    nwk_h = np.ascontiguousarray(nw.reshape(16, 128).T)
    nbk_h = np.ascontiguousarray(nb.reshape(16, 128).T)

    nc = build_program()

    in_maps = []
    d_ids = np.arange(D, dtype=np.int32)
    for i in range(NCORES):
        dsl = slice(i * DSL, (i + 1) * DSL)
        tsl = slice(i * TLOC, (i + 1) * TLOC)
        zw_rows = (d_ids // DSL) * (DSL * NCORES) + (d_ids % DSL) * NCORES + i
        zw = np.ascontiguousarray(zw_rows.astype(np.int32).reshape(16, 128).T)
        in_maps.append({
            "x_tok": np.ascontiguousarray(x[tsl, :]),
            "x_dsl": np.ascontiguousarray(xT[dsl, :]),
            "xpwT": xpwT_h,
            "routT": routT_h,
            "dtwT": np.ascontiguousarray(dtw.T[:, dsl]),
            "dtb2": np.ascontiguousarray(dtb[dsl].reshape(2, 128).T),
            "alog2": np.ascontiguousarray(
                alog[dsl].reshape(2, 128, NST).transpose(1, 0, 2).reshape(128, 2 * NST)),
            "dcol2": np.ascontiguousarray(dpar[dsl].reshape(2, 128).T),
            "nwsl": np.ascontiguousarray(nw[dsl].reshape(2, 128).T),
            "nbsl": np.ascontiguousarray(nb[dsl].reshape(2, 128).T),
            "nwk": nwk_h,
            "nbk": nbk_h,
            "outwT": outwT_h,
            "w1q": w1q_h,
            "w2q": w2q_h,
            "ident": ident_h,
            "triuT": triu_h,
            "onesq": ones_h,
            "ceoff": ceoff_h,
            "tokid": tokid_h,
            "zwidx": zw,
        })

    res = run_bass_kernel_spmd(nc, in_maps, core_ids=list(range(NCORES)))
    out = np.concatenate([res.results[i]["out_tok"] for i in range(NCORES)], axis=0)
    return out.reshape(1, L, D).astype(np.float32)


# version 14
# speedup vs baseline: 1.0418x; 1.0418x over previous
import numpy as np
import contextlib
import concourse.bass as bass
import concourse.mybir as mybir
from concourse.bass_utils import run_bass_kernel_spmd

F32 = mybir.dt.float32
I32 = mybir.dt.int32
AF = mybir.ActivationFunctionType
ALU = mybir.AluOpType
AX = mybir.AxisListType

NCORES = 8
L = 8192
D = 2048
NST = 64
RNK = 128
PER = 512
TLOC = L // NCORES
NTT = TLOC // 128
DSL = D // NCORES
E = 8
IFF = 1408
CAP = 384
TRASH = E * CAP
NSLOT = E * CAP + 128
FD = 1024
NCH = L // FD
PRJ = RNK + 2 * NST
BROW = 260
EPS = 1e-5


class Rec:
    ENGS = ("sync", "tensor", "vector", "scalar", "gpsimd")

    def __init__(self, nc, stack):
        self.nc = nc
        self.stack = stack
        self.q = {e: [] for e in self.ENGS}
        self.semh = {}
        self.semv = {}
        self.lastw = {e: {} for e in self.ENGS}
        for s in ("tc", "vc", "sc", "gc", "od"):
            self.ensure(s)

    def ensure(self, s):
        if s not in self.semh:
            self.semh[s] = self.stack.enter_context(self.nc.semaphore(s))
            self.semv[s] = 0

    def emit(self, eng, deps, build, inc, inck=1):
        self.ensure(inc)
        for s, _v in deps:
            self.ensure(s)
        deps2 = []
        for s, v in deps:
            if v <= 0:
                continue
            if self.lastw[eng].get(s, 0) >= v:
                continue
            self.lastw[eng][s] = v
            deps2.append((s, v))
        self.semv[inc] += inck
        self.q[eng].append((deps2, build, inc, inck))
        return self.semv[inc]

    def val(self, sem):
        return self.semv[sem]

    def snap(self):
        return dict(self.semv)

    def bar(self, snapshot, skip=()):
        return [(s, v) for s, v in snapshot.items() if s not in skip]

    def wait_only(self, eng, deps):
        for s, _v in deps:
            self.ensure(s)
        deps2 = []
        for s, v in deps:
            if v <= 0 or self.lastw[eng].get(s, 0) >= v:
                continue
            self.lastw[eng][s] = v
            deps2.append((s, v))
        if deps2:
            self.q[eng].append((deps2, None, None, 0))

    def replay(self, eng_name, eng):
        for deps, build, incs, inck in self.q[eng_name]:
            for s, v in deps:
                eng.wait_ge(self.semh[s], v)
            if build is not None:
                build(eng).then_inc(self.semh[incs], inck)


def build_program():
    nc = bass.Bass("TRN2", target_bir_lowering=False)
    stack = contextlib.ExitStack()

    def din(name, shape, dtype=F32):
        return nc.dram_tensor(name, list(shape), dtype, kind="ExternalInput").ap()

    x_tok = din("x_tok", (TLOC, D))
    x_dsl = din("x_dsl", (DSL, L))
    xpwT = din("xpwT", (D, PRJ))
    routT = din("routT", (D, E))
    dtwT = din("dtwT", (RNK, DSL))
    dtb2 = din("dtb2", (128, 2))
    alog2 = din("alog2", (128, 2 * NST))
    dcol2 = din("dcol2", (128, 2))
    nwsl = din("nwsl", (128, 2))
    nbsl = din("nbsl", (128, 2))
    nwk = din("nwk", (128, 16))
    nbk = din("nbk", (128, 16))
    outwT = din("outwT", (D, D))
    w1q = din("w1q", (E, 11, 128, 2, 16, 128))
    w2q = din("w2q", (E, 4, 11, 128, 512))
    ident = din("ident", (128, 128))
    triuT = din("triuT", (128, 128))
    onesq = din("onesq", (128, 128))
    ceoff = din("ceoff", (128, E))
    tokid = din("tokid", (128, NTT), I32)
    zwidx = din("zwidx", (128, 16), I32)

    out_tok = nc.dram_tensor("out_tok", [TLOC, D], F32, kind="ExternalOutput").ap()

    projb = nc.dram_tensor("projb", [BROW, TLOC], F32).ap()
    pag = nc.dram_tensor("pag", [NCORES * BROW, TLOC], F32).ap()
    zb = nc.dram_tensor("zb", [DSL, L], F32).ap()
    zag = nc.dram_tensor("zag", [D * NCORES, FD], F32).ap()
    xnb = nc.dram_tensor("xnb", [TLOC, D], F32).ap()
    toksD = nc.dram_tensor("toksD", [NSLOT, 1], I32).ap()
    y2w = nc.dram_tensor("y2w", [NSLOT, D], F32).ap()
    flsh = nc.dram_tensor("flsh", [4, 128], F32).ap()

    r = Rec(nc, stack)
    TC, VC, SC, GC, OD = "tc", "vc", "sc", "gc", "od"

    def dma(eng, sem, out, in_, deps=()):
        return r.emit(eng, deps, lambda e, o=out, i=in_: e.dma_start(out=o, in_=i),
                      inc=sem, inck=16)

    def sbuf(name, shape, dtype=F32, scope=None):
        return (scope or stack).enter_context(nc.sbuf_tensor(name, list(shape), dtype))

    def psum(name, scope):
        return scope.enter_context(nc.psum_tensor(name, [128, 512], F32))

    c_id = sbuf("c_id", (128, 128))
    c_ceoff = sbuf("c_ceoff", (128, E))
    c_zwidx = sbuf("c_zwidx", (128, 16), I32)
    s_xpw = sbuf("s_xpw", (128, 16 * PRJ))
    s_rout = sbuf("s_rout", (128, 16 * E))
    s_dtw = sbuf("s_dtw", (128, DSL))
    s_dtb = sbuf("s_dtb", (128, 2))
    s_alog = sbuf("s_alog", (128, 2 * NST))
    s_A = sbuf("s_A", (128, 2 * NST))
    s_dcol = sbuf("s_dcol", (128, 2))
    s_nwsl = sbuf("s_nwsl", (128, 2))
    s_nbsl = sbuf("s_nbsl", (128, 2))
    s_nwk = sbuf("s_nwk", (128, 16))
    s_nbk = sbuf("s_nbk", (128, 16))
    s_lg = sbuf("s_lg", (128, NTT * E))
    s_w12 = sbuf("s_w12", (128, 2 * NTT))
    s_msk = sbuf("s_msk", (128, NTT * E))
    s_m1 = sbuf("s_m1", (128, NTT * E))
    s_m2 = sbuf("s_m2", (128, NTT * E))
    s_sloti = sbuf("s_sloti", (128, 2 * NTT), I32)
    s_stoks = sbuf("s_stoks", (128, NSLOT // 128), I32)
    s_zero = sbuf("s_zero", (128, 128))
    s_col = sbuf("s_col", (128, 16))
    s_e8 = sbuf("s_e8", (128, 4 * E))

    dma("sync", "cst", c_id[:, :], ident)
    dma("sync", "cst", c_ceoff[:, :], ceoff)
    dma("sync", "cst", c_zwidx[:, :], zwidx)
    for k in range(16):
        dma("sync", "cst", s_xpw[:, k * PRJ:(k + 1) * PRJ], xpwT[k * 128:(k + 1) * 128, :])
        dma("sync", "cst", s_rout[:, k * E:(k + 1) * E], routT[k * 128:(k + 1) * 128, :])
    dma("sync", "cst", s_dtw[:, :], dtwT)
    dma("sync", "cst", s_dtb[:, :], dtb2)
    dma("sync", "cst", s_alog[:, :], alog2)
    dma("sync", "cst", s_dcol[:, :], dcol2)
    dma("sync", "cst", s_nwsl[:, :], nwsl)
    dma("sync", "cst", s_nbsl[:, :], nbsl)
    dma("sync", "cst", s_nwk[:, :], nwk)
    dma("sync", "cst", s_nbk[:, :], nbk)
    v_consts = r.val("cst")

    vs = r.emit("scalar", [("cst", v_consts)],
                lambda e: e.activation(s_A[:, :], s_alog[:, :], AF.Exp), inc=SC)
    vA = r.emit("vector", [(SC, vs)],
                lambda e: e.tensor_scalar_mul(s_A[:, :], s_A[:, :], -1.0), inc=VC)
    vz = r.emit("vector", [], lambda e: e.memset(s_zero[:, :], 0.0), inc=VC)
    for zi in range(16):
        dma("scalar", "tz", y2w[TRASH:TRASH + 128, zi * 128:(zi + 1) * 128],
            s_zero[:, :], deps=[(VC, vz)])
    for i in range(NSLOT // 128):
        dma("scalar", "tz", toksD[i * 128:(i + 1) * 128, :],
            s_zero[:, i:i + 1].bitcast(I32), deps=[(VC, vz)])
    v_toks_zero = r.val("tz")

    tok_scope = contextlib.ExitStack()
    c_triu = sbuf("c_triu", (128, 128), scope=tok_scope)
    c_ones = sbuf("c_ones", (128, 128), scope=tok_scope)
    c_tokid = sbuf("c_tokid", (128, NTT), I32, scope=tok_scope)
    s_strow = sbuf("s_strow", (4, TLOC), scope=tok_scope)
    A = sbuf("A", (128, 12544), scope=tok_scope)
    a_xt = A[:, 0:2048]
    a_sq = A[:, 2048:4096]
    a_xn = A[:, 4096:6144]
    a_x2 = A[:, 6144:8192]
    a_xnT = A[:, 8192:10240]
    a_x2T = A[:, 10240:12288]
    a_prj = A[:, 12288:12288 + 256]
    dma("sync", "ct2", c_triu[:, :], triuT)
    dma("sync", "ct2", c_ones[:, :], onesq)
    dma("sync", "ct2", c_tokid[:, :], tokid)
    v_ct2 = r.val("ct2")
    p_t1 = psum("p_t1", tok_scope)
    p_t2 = psum("p_t2", tok_scope)
    p_pr = psum("p_pr", tok_scope)
    p_lg = psum("p_lg", tok_scope)
    p_st = psum("p_st", tok_scope)
    p_rk = psum("p_rk", tok_scope)

    c0, c1, c2, c3 = s_col[:, 0:1], s_col[:, 1:2], s_col[:, 2:3], s_col[:, 3:4]
    mu1, rs1, mu2, rs2 = s_col[:, 4:5], s_col[:, 5:6], s_col[:, 6:7], s_col[:, 7:8]
    stc = s_col[:, 8:12]

    tc_e = sc_e = vc_e = gd_e = 0
    pt_free = [0, 0]
    pr_free = 0
    lg_free = 0
    prj_dma = 0
    v_xn_dma = []
    v_strow = 0
    v_logit = []

    def ln_stats(src, mu, rs, extra_v, extra_s):
        v = r.emit("vector", extra_v, lambda e, src=src: e.reduce_sum(c0, src, axis=AX.X),
                   inc=VC)
        s = r.emit("scalar", extra_s, lambda e, src=src: e.activation(
            a_sq, src, AF.Square), inc=SC)
        v = r.emit("vector", [(SC, s), (VC, v)],
                   lambda e: e.reduce_sum(c1, a_sq, axis=AX.X), inc=VC)
        v = r.emit("vector", [(VC, v)],
                   lambda e, mu=mu: e.tensor_scalar_mul(mu, c0, 1.0 / D), inc=VC)
        v = r.emit("vector", [(VC, v)],
                   lambda e: e.tensor_scalar_mul(c2, c1, 1.0 / D), inc=VC)
        v = r.emit("vector", [(VC, v)],
                   lambda e, mu=mu: e.tensor_mul(c3, mu, mu), inc=VC)
        v = r.emit("vector", [(VC, v)],
                   lambda e: e.tensor_sub(c3, c2, c3), inc=VC)
        v = r.emit("vector", [(VC, v)],
                   lambda e: e.tensor_scalar_add(c3, c3, EPS), inc=VC)
        s = r.emit("scalar", [(VC, v)],
                   lambda e: e.activation(c3, c3, AF.Sqrt), inc=SC)
        v = r.emit("vector", [(SC, s)],
                   lambda e, rs=rs: e.reciprocal(rs, c3), inc=VC)
        return v

    for tt in range(NTT):
        vxd = dma("sync", "xl", a_xt, x_tok[tt * 128:(tt + 1) * 128, :],
                  deps=[(VC, vc_e), (SC, sc_e)])
        v = ln_stats(a_xt, mu1, rs1, [("xl", vxd)], [("xl", vxd), (VC, vc_e)])
        v_xn = r.emit("vector", [(VC, v), (TC, tc_e), ("xnb", gd_e)],
                      lambda e: e.tensor_scalar(a_xn, a_xt, mu1, rs1,
                                                ALU.subtract, ALU.mult), inc=VC)
        v = ln_stats(a_xn, mu2, rs2, [(VC, v_xn)], [(VC, v_xn)])
        v_x2 = r.emit("vector", [(VC, v), (TC, tc_e)],
                      lambda e: e.tensor_scalar(a_x2, a_xn, mu2, rs2,
                                                ALU.subtract, ALU.mult), inc=VC)
        v_xn_dma.append(dma("gpsimd", "xnb", xnb[tt * 128:(tt + 1) * 128, :], a_xn,
                            deps=[(VC, v_xn)]))
        va = r.emit("vector", [(VC, v_x2)],
                    lambda e: e.tensor_copy(s_col[:, 8:9], rs1), inc=VC)
        va = r.emit("vector", [(VC, va)],
                    lambda e: e.tensor_mul(s_col[:, 9:10], mu1, rs1), inc=VC)
        va = r.emit("vector", [(VC, va)],
                    lambda e: e.tensor_copy(s_col[:, 10:11], rs2), inc=VC)
        va = r.emit("vector", [(VC, va)],
                    lambda e: e.tensor_mul(s_col[:, 11:12], mu2, rs2), inc=VC)
        vt = r.emit("tensor", [(VC, va), ("cst", v_consts), (SC, v_strow)],
                    lambda e: e.transpose(p_st[0:4, 0:128], stc, c_id[:, :]), inc=TC)
        v_strow = r.emit("scalar", [(TC, vt)],
                         lambda e, tt=tt: e.activation(
                             s_strow[0:4, tt * 128:(tt + 1) * 128],
                             p_st[0:4, 0:128], AF.Copy), inc=SC)
        v_evac = 0
        for src, dst, scaled, vdep in ((a_xn, a_xnT, False, v_xn),
                                       (a_x2, a_x2T, True, v_x2)):
            for k in range(16):
                pi = k % 2
                pp = p_t1 if pi == 0 else p_t2
                vt = r.emit("tensor", [(VC, vdep), (SC, pt_free[pi])],
                            lambda e, pp=pp, src=src, k=k: e.transpose(
                                pp[:, 0:128], src[:, k * 128:(k + 1) * 128],
                                c_id[:, :]), inc=TC)
                v_evac = r.emit(
                    "scalar", [(TC, vt), ("cst", v_consts)],
                    (lambda e, pp=pp, dst=dst, k=k: e.activation(
                        dst[:, k * 128:(k + 1) * 128], pp[:, 0:128], AF.Identity,
                        bias=s_nbk[:, k:k + 1], scale=s_nwk[:, k:k + 1]))
                    if scaled else
                    (lambda e, pp=pp, dst=dst, k=k: e.activation(
                        dst[:, k * 128:(k + 1) * 128], pp[:, 0:128], AF.Copy)),
                    inc=SC)
                pt_free[pi] = v_evac
        for pt in range(2):
            for k in range(16):
                vt = r.emit("tensor",
                            [(SC, v_evac)] + ([(SC, pr_free)] if k == 0 else []),
                            lambda e, pt=pt, k=k: e.matmul(
                                p_pr[:, 0:128],
                                s_xpw[:, k * PRJ + pt * 128:k * PRJ + (pt + 1) * 128],
                                a_x2T[:, k * 128:(k + 1) * 128],
                                start=(k == 0), stop=(k == 15)), inc=TC)
            pr_free = r.emit("scalar", [(TC, vt), ("prj", prj_dma)],
                             lambda e, pt=pt: e.activation(
                                 a_prj[:, 0:128], p_pr[:, 0:128], AF.Copy), inc=SC)
            prj_dma = dma("gpsimd", "prj",
                          projb[pt * 128:(pt + 1) * 128, tt * 128:(tt + 1) * 128],
                          a_prj[:, 0:128], deps=[(SC, pr_free)])
        for k in range(16):
            vt = r.emit("tensor",
                        [(SC, v_evac)] + ([(SC, lg_free)] if k == 0 else []),
                        lambda e, k=k: e.matmul(
                            p_lg[:, 0:E], a_xnT[:, k * 128:(k + 1) * 128],
                            s_rout[:, k * E:(k + 1) * E],
                            start=(k == 0), stop=(k == 15)), inc=TC)
        lg_free = r.emit("scalar", [(TC, vt)],
                         lambda e, tt=tt: e.activation(
                             s_lg[:, tt * E:(tt + 1) * E], p_lg[:, 0:E], AF.Copy),
                         inc=SC)
        v_logit.append(lg_free)
        tc_e, sc_e, vc_e, gd_e = r.val(TC), r.val(SC), r.val(VC), r.val("xnb")

    e0, e1t, e2t, e3t = (s_e8[:, 0:E], s_e8[:, E:2 * E], s_e8[:, 2 * E:3 * E],
                         s_e8[:, 3 * E:4 * E])

    def cumsum8(src, t1, t2, dep):
        v = r.emit("vector", [(VC, dep)], lambda e: e.tensor_copy(t1, src), inc=VC)
        v = r.emit("vector", [(VC, v)],
                   lambda e: e.tensor_add(t1[:, 1:8], src[:, 1:8], src[:, 0:7]), inc=VC)
        v = r.emit("vector", [(VC, v)], lambda e: e.tensor_copy(t2, t1), inc=VC)
        v = r.emit("vector", [(VC, v)],
                   lambda e: e.tensor_add(t2[:, 2:8], t1[:, 2:8], t1[:, 0:6]), inc=VC)
        v = r.emit("vector", [(VC, v)], lambda e: e.tensor_copy(t1, t2), inc=VC)
        v = r.emit("vector", [(VC, v)],
                   lambda e: e.tensor_add(t1[:, 4:8], t2[:, 4:8], t2[:, 0:4]), inc=VC)
        return t1, v

    for tt in range(NTT):
        lg = s_lg[:, tt * E:(tt + 1) * E]
        m1 = s_m1[:, tt * E:(tt + 1) * E]
        m2 = s_m2[:, tt * E:(tt + 1) * E]
        v = r.emit("vector", [(SC, v_logit[tt])],
                   lambda e, lg=lg: e.reduce_max(c0, lg, axis=AX.X), inc=VC)
        v = r.emit("vector", [(VC, v)],
                   lambda e, lg=lg: e.tensor_single_scalar(e0, lg, c0, ALU.is_equal),
                   inc=VC)
        cs, v = cumsum8(e0, e1t, e2t, v)
        v = r.emit("vector", [(VC, v)],
                   lambda e, cs=cs: e.tensor_single_scalar(e2t, cs, 1.0, ALU.is_equal),
                   inc=VC)
        v = r.emit("vector", [(VC, v)], lambda e, m1=m1: e.tensor_mul(m1, e0, e2t),
                   inc=VC)
        v = r.emit("vector", [(VC, v)],
                   lambda e, m1=m1: e.tensor_scalar_mul(e0, m1, 1e30), inc=VC)
        v = r.emit("vector", [(VC, v)], lambda e, lg=lg: e.tensor_sub(e0, lg, e0),
                   inc=VC)
        v = r.emit("vector", [(VC, v)], lambda e: e.reduce_max(c1, e0, axis=AX.X),
                   inc=VC)
        v = r.emit("vector", [(VC, v)],
                   lambda e: e.tensor_single_scalar(e1t, e0, c1, ALU.is_equal), inc=VC)
        cs2, v = cumsum8(e1t, e2t, e3t, v)
        v = r.emit("vector", [(VC, v)],
                   lambda e, cs2=cs2: e.tensor_single_scalar(e3t, cs2, 1.0,
                                                             ALU.is_equal), inc=VC)
        v = r.emit("vector", [(VC, v)], lambda e, m2=m2: e.tensor_mul(m2, e1t, e3t),
                   inc=VC)
        v = r.emit("vector", [(VC, v)], lambda e: e.tensor_sub(c2, c1, c0), inc=VC)
        s = r.emit("scalar", [(VC, v)], lambda e: e.activation(c3, c2, AF.Exp), inc=SC)
        v = r.emit("vector", [(SC, s)], lambda e: e.tensor_scalar_add(c2, c3, 1.0),
                   inc=VC)
        v = r.emit("vector", [(VC, v)],
                   lambda e, tt=tt: e.reciprocal(s_w12[:, 2 * tt:2 * tt + 1], c2),
                   inc=VC)
        v = r.emit("vector", [(VC, v)],
                   lambda e, tt=tt: e.tensor_mul(s_w12[:, 2 * tt + 1:2 * tt + 2], c3,
                                                 s_w12[:, 2 * tt:2 * tt + 1]), inc=VC)
        v = r.emit("vector", [(VC, v)],
                   lambda e, m1=m1, m2=m2, tt=tt: e.tensor_add(
                       s_msk[:, tt * E:(tt + 1) * E], m1, m2), inc=VC)
    v_masks = r.val(VC)

    rk_free = 0
    v_slots = 0
    for tt in range(NTT):
        for j in range(tt + 1):
            lhs = c_ones if j < tt else c_triu
            deps = [(VC, v_masks), ("ct2", v_ct2)]
            if j == 0:
                deps.append((SC, rk_free))
            vt = r.emit("tensor", deps,
                        lambda e, j=j, tt=tt, lhs=lhs: e.matmul(
                            p_rk[:, 0:E], lhs[:, :], s_msk[:, j * E:(j + 1) * E],
                            start=(j == 0), stop=(j == tt)), inc=TC)
        rk_free = r.emit("scalar", [(TC, vt), (VC, v_masks)],
                         lambda e, tt=tt: e.activation(
                             s_lg[:, tt * E:(tt + 1) * E], p_rk[:, 0:E], AF.Copy),
                         inc=SC)
        rk = s_lg[:, tt * E:(tt + 1) * E]
        v = r.emit("vector", [(SC, rk_free), ("cst", v_consts)],
                   lambda e, rk=rk: e.tensor_add(e0, rk, c_ceoff[:, :]), inc=VC)
        v = r.emit("vector", [(VC, v)],
                   lambda e, rk=rk: e.tensor_single_scalar(
                       e1t, rk, float(CAP) + 0.5, ALU.is_gt), inc=VC)
        v = r.emit("vector", [(VC, v)],
                   lambda e: e.tensor_scalar(e2t, e0, -1.0, float(TRASH),
                                             ALU.mult, ALU.add), inc=VC)
        v = r.emit("vector", [(VC, v)], lambda e: e.tensor_mul(e2t, e2t, e1t), inc=VC)
        v = r.emit("vector", [(VC, v)], lambda e: e.tensor_add(e0, e0, e2t), inc=VC)
        v = r.emit("vector", [(VC, v)],
                   lambda e, tt=tt: e.tensor_mul(e2t, s_m1[:, tt * E:(tt + 1) * E], e0),
                   inc=VC)
        v = r.emit("vector", [(VC, v)],
                   lambda e: e.reduce_sum(c0, e2t, axis=AX.X), inc=VC)
        v = r.emit("vector", [(VC, v)],
                   lambda e, tt=tt: e.tensor_mul(e2t, s_m2[:, tt * E:(tt + 1) * E], e0),
                   inc=VC)
        v = r.emit("vector", [(VC, v)],
                   lambda e: e.reduce_sum(c1, e2t, axis=AX.X), inc=VC)
        v = r.emit("vector", [(VC, v)],
                   lambda e, tt=tt: e.tensor_copy(s_sloti[:, 2 * tt:2 * tt + 1], c0),
                   inc=VC)
        v_slots = r.emit("vector", [(VC, v)],
                         lambda e, tt=tt: e.tensor_copy(
                             s_sloti[:, 2 * tt + 1:2 * tt + 2], c1), inc=VC)

    for tt in range(NTT):
        for si in range(2):
            r.emit("gpsimd", [(VC, v_slots), ("tz", v_toks_zero), ("ct2", v_ct2)],
                   lambda e, tt=tt, si=si: e.indirect_dma_start(
                       out=toksD, out_offset=bass.IndirectOffsetOnAxis(
                           ap=s_sloti[:, 2 * tt + si:2 * tt + si + 1], axis=0),
                       in_=c_tokid[:, tt:tt + 1], in_offset=None),
                   inc="sca", inck=16)
    v_scat = dma("gpsimd", "sca", flsh[0:1, 0:1], toksD[0:1, 0:1].bitcast(F32),
                 deps=[("sca", r.val("sca"))])
    for cc in range(NSLOT // 128):
        vtk = dma("sync", "stk", s_stoks[:, cc:cc + 1],
                  toksD[cc * 128:(cc + 1) * 128, :], deps=[("sca", v_scat)])

    dma("gpsimd", "prj", projb[256:260, :], s_strow[:, :], deps=[(SC, v_strow)])
    v_projb = r.val("prj")

    r.wait_only("gpsimd", [("prj", v_projb)])
    r.emit("gpsimd", [],
           lambda e: e.collective_compute(
               "AllGather", ALU.bypass, replica_groups=[list(range(NCORES))],
               ins=[projb.opt()], outs=[pag.opt()]), inc=GC)
    v_ag = r.val(GC)
    tok_scope.close()
    tok_snap = r.snap()
    barT = r.bar(tok_snap, skip=("od",))

    moe_scope = contextlib.ExitStack()
    b_xgtm = sbuf("b_xgtm", (128, 2048), scope=moe_scope)
    b_xgfm = sbuf("b_xgfm", (128, 16 * CAP), scope=moe_scope)
    s_yact = sbuf("s_yact", (128, 11 * CAP), scope=moe_scope)
    s_w1 = sbuf("s_w1", (128, 3 * 2048), scope=moe_scope)
    s_w2 = sbuf("s_w2", (128, 4 * 512), scope=moe_scope)
    s_sg = sbuf("s_sg", (128, CAP), scope=moe_scope)
    s_y2s = sbuf("s_y2s", (128, 2 * 512), scope=moe_scope)

    scan_scope = contextlib.ExitStack()
    s_dr = sbuf("s_dr", (128, FD), scope=scan_scope)
    s_stb = sbuf("s_stb", (128, 4 * FD), scope=scan_scope)
    s_xc = sbuf("s_xc", (128, FD), scope=scan_scope)
    s_t0 = sbuf("s_t0", (128, FD), scope=scan_scope)
    s_dl = sbuf("s_dl", (128, 2 * FD), scope=scan_scope)
    s_dx = sbuf("s_dx", (128, 2 * FD), scope=scan_scope)
    s_ysb = sbuf("s_ysb", (128, 2 * FD), scope=scan_scope)
    s_dA = sbuf("s_dA", (128, 2 * FD), scope=scan_scope)
    s_u = sbuf("s_u", (128, 2 * FD), scope=scan_scope)
    s_h = sbuf("s_h", (128, 2 * FD), scope=scan_scope)
    s_tmp = sbuf("s_tmp", (128, 2 * FD), scope=scan_scope)
    s_bbc = sbuf("s_bbc", (128, 2 * FD), scope=scan_scope)
    s_cbc = sbuf("s_cbc", (128, 2 * FD), scope=scan_scope)
    s_state = sbuf("s_state", (128, 2 * NST), scope=scan_scope)

    p_gu = psum("p_gu", moe_scope)
    p_m2 = [psum(f"p_m2_{m}", moe_scope) for m in range(2)]
    p_dt = psum("p_dt", scan_scope)
    p_y = [scan_scope.enter_context(nc.psum_tensor(f"p_y{g}", [128, FD], F32))
           for g in range(2)]

    w1_free = [0, 0, 0]
    w2_free = [0, 0, 0, 0]
    w1s = ["w1a", "w1b", "w1c"]
    w2s = ["w2a", "w2b", "w2c", "w2d"]
    y2s = ["y2a", "y2b"]
    y2s_free = [0, 0]
    pm_free = [0, 0]
    v_xn_all = v_xn_dma[-1]

    def moe_units():
        pgu_free = 0
        pgu_free_sc = 0
        v_yact = 0
        v_tr = 0
        xg_free = 0
        it_w1 = 0
        for ee in range(E):
            for mc in range(3):
                vg = r.emit(
                    "gpsimd", [("stk", vtk), ("xnb", v_xn_all), (TC, xg_free)],
                    lambda e, ee=ee, mc=mc: e.indirect_dma_start(
                        out=b_xgtm[:, :], out_offset=None,
                        in_=xnb, in_offset=bass.IndirectOffsetOnAxis(
                            ap=s_stoks[:, ee * 3 + mc:ee * 3 + mc + 1], axis=0)),
                    inc="mg", inck=16)
                vg = dma("gpsimd", "mg", flsh[1:2, :], b_xgtm[0:1, 0:128],
                         deps=[("mg", vg)])
                for k in range(16):
                    deps = [("mg", vg), ("cst", v_consts), (SC, v_tr)]
                    if k == 0:
                        deps += [(VC, pgu_free), (SC, pgu_free_sc)]
                    vt = r.emit("tensor", deps,
                                lambda e, k=k: e.transpose(
                                    p_gu[:, 0:128],
                                    b_xgtm[:, k * 128:(k + 1) * 128],
                                    c_id[:, :]), inc=TC)
                    v_tr = r.emit("scalar", [(TC, vt)],
                                  lambda e, k=k, mc=mc: e.activation(
                                      b_xgfm[:, k * CAP + mc * 128:
                                             k * CAP + (mc + 1) * 128],
                                      p_gu[:, 0:128], AF.Copy), inc=SC)
                xg_free = r.val(TC)
                pgu_free_sc = v_tr
                yield None
            for iu in range(11):
                for j in range(2):
                    bufi = it_w1 % 3
                    it_w1 += 1
                    vw = dma("scalar", w1s[bufi],
                             s_w1[:, bufi * 2048:(bufi + 1) * 2048],
                             w1q[ee, iu, :, j].rearrange("dd k ii -> dd (k ii)"),
                             deps=[(TC, w1_free[bufi])] +
                             (barT if (ee == 0 and iu == 0) else []))
                    for k in range(16):
                        deps = [(w1s[bufi], vw), (SC, v_tr)]
                        if k == 0:
                            deps += [(SC, pgu_free_sc)] if j == 0 else []
                        vt = r.emit("tensor", deps,
                                    lambda e, k=k, bufi=bufi: e.matmul(
                                        p_gu[:, 0:CAP],
                                        s_w1[:, bufi * 2048 + k * 128:
                                             bufi * 2048 + (k + 1) * 128],
                                        b_xgfm[:, k * CAP:(k + 1) * CAP],
                                        start=(k == 0), stop=(k == 15)), inc=TC)
                    w1_free[bufi] = vt
                    if j == 0:
                        pgu_free_sc = r.emit(
                            "scalar", [(TC, vt), (VC, pgu_free)],
                            lambda e: e.activation(s_sg[:, :], p_gu[:, 0:CAP],
                                                   AF.Silu), inc=SC)
                    else:
                        v_yact = r.emit("vector", [(TC, vt), (SC, pgu_free_sc)],
                                        lambda e, iu=iu: e.tensor_mul(
                                            s_yact[:, iu * CAP:(iu + 1) * CAP],
                                            s_sg[:, :], p_gu[:, 0:CAP]), inc=VC)
                        pgu_free = v_yact
                    yield None
            for dc in range(4):
                for ph, ms in ((0, (0, 1)), (1, (2,))):
                    for k in range(11):
                        bufi = k % 4
                        vw2 = dma("scalar", w2s[bufi],
                                  s_w2[:, bufi * 512:(bufi + 1) * 512],
                                  w2q[ee, dc, k], deps=[(TC, w2_free[bufi])])
                        for mi, m in enumerate(ms):
                            deps = [(w2s[bufi], vw2), (VC, v_yact)]
                            if k == 0:
                                deps.append((SC, pm_free[mi]))
                            vt = r.emit("tensor", deps,
                                        lambda e, m=m, k=k, bufi=bufi, mi=mi:
                                        e.matmul(
                                            p_m2[mi][:, :],
                                            s_yact[:, k * CAP + m * 128:
                                                   k * CAP + (m + 1) * 128],
                                            s_w2[:, bufi * 512:(bufi + 1) * 512],
                                            start=(k == 0), stop=(k == 10)),
                                        inc=TC)
                        w2_free[bufi] = vt
                    for mi, m in enumerate(ms):
                        sbi = m % 2
                        pm_free[mi] = r.emit(
                            "scalar", [(TC, w2_free[(10) % 4]),
                                       (y2s[sbi], y2s_free[sbi])],
                            lambda e, mi=mi, sbi=sbi: e.activation(
                                s_y2s[:, sbi * 512:(sbi + 1) * 512],
                                p_m2[mi][:, :], AF.Copy), inc=SC)
                        y2s_free[sbi] = dma(
                            "scalar", y2s[sbi],
                            y2w[ee * CAP + m * 128:ee * CAP + (m + 1) * 128,
                                dc * 512:(dc + 1) * 512],
                            s_y2s[:, sbi * 512:(sbi + 1) * 512],
                            deps=[(SC, pm_free[mi])])
                    yield None

    moe_gen = moe_units()

    SL = {}
    for nm, t in (("dl", s_dl), ("dx", s_dx), ("ysb", s_ysb)):
        SL[nm] = [t[:, 0:FD], t[:, FD:2 * FD]]
    RG = {}
    for nm, t, nr in (("dA", s_dA, 2), ("u", s_u, 2), ("h", s_h, 2),
                      ("tmp", s_tmp, 2), ("bbc", s_bbc, 2), ("cbc", s_cbc, 2)):
        RG[nm] = [t[:, i * FD:(i + 1) * FD] for i in range(nr)]
    stbs = [s_stb[:, j * FD:(j + 1) * FD] for j in range(4)]

    bbc_used = [0, 0]
    cbc_used = [0, 0]
    dA_used = [0, 0]
    u_used = [0, 0]
    h_free_sc = [0, 0]
    h_pool = [0, 0]
    tmp_used = [0, 0]
    y_evac = [0, 0]
    st_sc = [[0] * NST, [0] * NST]
    v_zb = [0, 0]
    zbs = ["z0", "z1"]
    bsem = ["b0", "b1"]
    csem = ["ca", "cb"]
    stb_free = 0
    dr_free = 0
    xc_free = 0
    vy_prev = [0, 0]
    pdt_free = 0
    t0_sc = 0
    it = 0

    for c in range(NCH):
        blk = c * BROW
        first = barT if c == 0 else []
        v_stb = 0
        for j in range(4):
            v_stb = dma("sync", "stb", stbs[j],
                        pag[blk + 256 + j:blk + 257 + j, :].to_broadcast([128, FD]),
                        deps=first + [(GC, v_ag), (VC, stb_free)])
        v_dr = dma("sync", "dr", s_dr[:, :], pag[blk:blk + 128, :],
                   deps=first + [(GC, v_ag), (TC, dr_free)])
        vdl = [0, 0]
        for g in range(2):
            vxc = dma("sync", "xc", s_xc[:, :],
                      x_dsl[g * 128:(g + 1) * 128, c * FD:(c + 1) * FD],
                      deps=first + [(VC, xc_free)])
            v = r.emit("vector", [("xc", vxc), ("stb", v_stb), (SC, vy_prev[g]),
                                  (SC, t0_sc)],
                       lambda e: e.tensor_mul(s_t0[:, :], s_xc[:, :], stbs[0]), inc=VC)
            xc_free = v
            v_xno = r.emit("vector", [(VC, v)],
                           lambda e: e.tensor_sub(s_t0[:, :], s_t0[:, :], stbs[1]),
                           inc=VC)
            vy = r.emit("scalar", [(VC, v_xno), ("cst", v_consts),
                                   (zbs[g], v_zb[g])],
                        lambda e, g=g: e.activation(SL["ysb"][g], s_t0[:, :], AF.Copy,
                                                    scale=s_dcol[:, g:g + 1]), inc=SC)
            vy_prev[g] = vy
            v = r.emit("vector", [(VC, v_xno)],
                       lambda e, g=g: e.tensor_mul(SL["dx"][g], s_t0[:, :], stbs[2]),
                       inc=VC)
            v = r.emit("vector", [(VC, v)],
                       lambda e, g=g: e.tensor_sub(SL["dx"][g], SL["dx"][g], stbs[3]),
                       inc=VC)
            vs2 = r.emit("scalar", [(VC, v), ("cst", v_consts)],
                         lambda e, g=g: e.activation(SL["dx"][g], SL["dx"][g],
                                                     AF.Identity,
                                                     bias=s_nbsl[:, g:g + 1],
                                                     scale=s_nwsl[:, g:g + 1]), inc=SC)
            va = 0
            for f in range(2):
                vt = r.emit("tensor",
                            [("dr", v_dr), ("cst", v_consts),
                             (VC, va if f == 1 else pdt_free)],
                            lambda e, g=g, f=f: e.matmul(
                                p_dt[:, :], s_dtw[:, g * 128:(g + 1) * 128],
                                s_dr[:, f * 512:(f + 1) * 512],
                                start=True, stop=True), inc=TC)
                dr_free = vt
                va = r.emit("vector", [(TC, vt), ("cst", v_consts), (SC, vy)],
                            lambda e, f=f, g=g: e.tensor_scalar_add(
                                s_t0[:, f * 512:(f + 1) * 512], p_dt[:, :],
                                s_dtb[:, g:g + 1]), inc=VC)
            pdt_free = va
            v = r.emit("vector", [(VC, va)],
                       lambda e: e.tensor_scalar_mul(s_xc[:, :], s_t0[:, :], -1.0),
                       inc=VC)
            v = r.emit("vector", [(VC, v)],
                       lambda e: e.tensor_tensor(s_xc[:, :], s_xc[:, :], s_t0[:, :],
                                                 ALU.min), inc=VC)
            vs = r.emit("scalar", [(VC, v)],
                        lambda e: e.activation(s_xc[:, :], s_xc[:, :], AF.Exp), inc=SC)
            vs = r.emit("scalar", [(SC, vs)],
                        lambda e: e.activation(s_xc[:, :], s_xc[:, :], AF.Ln, bias=1.0),
                        inc=SC)
            vs = r.emit("scalar", [(SC, vs)],
                        lambda e, g=g: e.activation(SL["dl"][g], s_t0[:, :], AF.Relu),
                        inc=SC)
            t0_sc = vs
            v = r.emit("vector", [(SC, vs)],
                       lambda e, g=g: e.tensor_add(SL["dl"][g], SL["dl"][g],
                                                   s_xc[:, :]), inc=VC)
            xc_free = v
            vdl[g] = v
            r.emit("vector", [(VC, v), (SC, vs2)],
                   lambda e, g=g: e.tensor_mul(SL["dx"][g], SL["dx"][g], SL["dl"][g]),
                   inc=VC)
        v_prep = r.val(VC)
        stb_free = v_prep

        for n in range(NST):
            bi = (c * NST + n) % 2
            ci = (c * NST + n) % 2
            vb = dma("sync", bsem[bi], RG["bbc"][bi],
                     pag[blk + 128 + n:blk + 129 + n, :].to_broadcast([128, FD]),
                     deps=[(VC, bbc_used[bi]), (GC, v_ag)])
            vc1 = dma("sync", csem[ci], RG["cbc"][ci][:, 0:PER],
                      pag[192 + n:193 + n, 0:PER].to_broadcast([128, PER]),
                      deps=[(GC, cbc_used[ci]), (GC, v_ag)])
            vc1 = dma("sync", csem[ci], RG["cbc"][ci][:, PER:FD],
                      pag[192 + n:193 + n, 0:PER].to_broadcast([128, PER]),
                      deps=[(GC, v_ag)])
            for g in range(2):
                di = it % 2
                vsA = r.emit("scalar",
                             [(VC, max(vdl[g], dA_used[di])), (VC, vA)],
                             lambda e, g=g, n=n, di=di: e.activation(
                                 RG["dA"][di], SL["dl"][g], AF.Exp,
                                 scale=s_A[:, g * NST + n:g * NST + n + 1]), inc=SC)
                vu = r.emit("vector", [(bsem[bi], vb),
                                       (VC, max(u_used[di], v_prep))],
                            lambda e, g=g, di=di, bi=bi: e.tensor_mul(
                                RG["u"][di], SL["dx"][g], RG["bbc"][bi]), inc=VC)
                bbc_used[bi] = vu
                hi = it % 2
                init = 0.0 if c == 0 else s_state[:, g * NST + n:g * NST + n + 1]
                vh = r.emit("vector",
                            [(SC, vsA), (VC, vu), (SC, h_free_sc[hi]),
                             (GC, h_pool[hi])] +
                            ([(SC, st_sc[g][n])] if c > 0 else []),
                            lambda e, hi=hi, di=di, init=init: e.tensor_tensor_scan(
                                RG["h"][hi], RG["dA"][di], RG["u"][di], init,
                                ALU.mult, ALU.add), inc=VC)
                dA_used[di] = vh
                u_used[di] = vh
                st_sc[g][n] = r.emit("scalar", [(VC, vh)],
                                     lambda e, hi=hi, g=g, n=n: e.activation(
                                         s_state[:, g * NST + n:g * NST + n + 1],
                                         RG["h"][hi][:, FD - 1:FD], AF.Copy), inc=SC)
                h_free_sc[hi] = st_sc[g][n]
                ti = it % 2
                vtm = r.emit("gpsimd", [(csem[ci], vc1), (TC, tmp_used[ti]),
                                        (VC, vh)],
                             lambda e, hi=hi, ti=ti, ci=ci: e.tensor_mul(
                                 RG["tmp"][ti], RG["h"][hi], RG["cbc"][ci]), inc=GC)
                cbc_used[ci] = vtm
                h_pool[hi] = vtm
                if n == 0:
                    vt = r.emit("tensor", [(SC, vy_prev[g]), ("cst", v_consts)],
                                lambda e, g=g: e.matmul(
                                    p_y[g][:, 0:512], c_id[:, :],
                                    SL["ysb"][g][:, 0:512],
                                    start=True, stop=False), inc=TC)
                    vt = r.emit("tensor", [],
                                lambda e, g=g: e.matmul(
                                    p_y[g][:, 512:FD], c_id[:, :],
                                    SL["ysb"][g][:, 512:FD],
                                    start=True, stop=False), inc=TC)
                for half in range(2):
                    vt = r.emit("tensor", [(GC, vtm)],
                                lambda e, g=g, ti=ti, half=half, n=n: e.matmul(
                                    p_y[g][:, half * 512:(half + 1) * 512],
                                    c_id[:, :],
                                    RG["tmp"][ti][:, half * 512:(half + 1) * 512],
                                    start=False, stop=(n == NST - 1)), inc=TC)
                tmp_used[ti] = vt
                it += 1
                if it % 4 == 0:
                    next(moe_gen, None)
        for g in range(2):
            ve = r.emit("scalar", [(TC, tmp_used[(it - 1) % 2]),
                                   (zbs[g], v_zb[g])],
                        lambda e, g=g: e.activation(SL["ysb"][g], p_y[g][:, :],
                                                    AF.Copy), inc=SC)
            y_evac[g] = ve
            v_zb[g] = dma("sync", zbs[g],
                          zb[g * 128:(g + 1) * 128, c * FD:(c + 1) * FD],
                          SL["ysb"][g], deps=[(SC, ve)])
    for _ in moe_gen:
        pass
    scan_scope.close()
    moe_scope.close()

    r.wait_only("gpsimd", [(zbs[0], v_zb[0]), (zbs[1], v_zb[1])])
    r.emit("gpsimd", [],
           lambda e: e.collective_compute(
               "AllGather", ALU.bypass, replica_groups=[list(range(NCORES))],
               ins=[zb.opt()], outs=[zag.opt()]), inc=GC)
    v_zag = r.val(GC)
    scan_snap = r.snap()
    barF = r.bar(scan_snap, skip=("od",))

    fin_scope = contextlib.ExitStack()
    b_acc = sbuf("b_acc", (128, NTT * 2048), scope=fin_scope)
    b_g1 = sbuf("b_g1", (128, 2048), scope=fin_scope)
    b_g2 = sbuf("b_g2", (128, 2048), scope=fin_scope)
    b_xr = sbuf("b_xr", (128, 2048), scope=fin_scope)
    zA = sbuf("zA", (128, 16 * FD), scope=fin_scope)
    s_ow = sbuf("s_ow", (128, 2 * 512), scope=fin_scope)
    p_att = [psum(f"p_att{i}", fin_scope) for i in range(8)]

    v_zg = []
    for k in range(16):
        v_zg.append(r.emit(
            "gpsimd", [(GC, v_zag), ("cst", v_consts)] + (barF if k == 0 else []),
            lambda e, k=k: e.indirect_dma_start(
                out=zA[:, k * FD:(k + 1) * FD], out_offset=None, in_=zag,
                in_offset=bass.IndirectOffsetOnAxis(ap=c_zwidx[:, k:k + 1], axis=0)),
            inc="zg", inck=16))
    v_zgf = dma("gpsimd", "zg", flsh[2:3, :], zA[0:1, 0:128], deps=[("zg", v_zg[15])])
    v_zg = [v_zgf] * 16

    v_acc = []
    for tt in range(NTT):
        vg1 = r.emit("gpsimd",
                     [(y2s[0], y2s_free[0]), (y2s[1], y2s_free[1]),
                      (VC, v_slots)] +
                     ([(VC, v_acc[-1])] if v_acc else barF),
                     lambda e, tt=tt: e.indirect_dma_start(
                         out=b_g1[:, :], out_offset=None, in_=y2w,
                         in_offset=bass.IndirectOffsetOnAxis(
                             ap=s_sloti[:, 2 * tt:2 * tt + 1], axis=0)),
                     inc="fg", inck=16)
        vg2 = r.emit("gpsimd", [],
                     lambda e, tt=tt: e.indirect_dma_start(
                         out=b_g2[:, :], out_offset=None, in_=y2w,
                         in_offset=bass.IndirectOffsetOnAxis(
                             ap=s_sloti[:, 2 * tt + 1:2 * tt + 2], axis=0)),
                     inc="fg", inck=16)
        vg2 = dma("gpsimd", "fg", flsh[3:4, :], b_g2[0:1, 0:128], deps=[("fg", vg2)])
        vg1 = vg2
        vxr = dma("sync", "xr", b_xr[:, :], x_tok[tt * 128:(tt + 1) * 128, :],
                  deps=([(VC, v_acc[-1])] if v_acc else barF))
        acc = b_acc[:, tt * 2048:(tt + 1) * 2048]
        v = r.emit("vector", [("fg", vg1)] + (barF if tt == 0 else []),
                   lambda e, acc=acc, tt=tt: e.tensor_scalar(
                       acc, b_g1[:, :], s_w12[:, 2 * tt:2 * tt + 1], None, ALU.mult),
                   inc=VC)
        v = r.emit("vector", [("fg", vg2), (VC, v)],
                   lambda e, acc=acc, tt=tt: e.scalar_tensor_tensor(
                       acc, b_g2[:, :], s_w12[:, 2 * tt + 1:2 * tt + 2], acc,
                       ALU.mult, ALU.add), inc=VC)
        v = r.emit("vector", [("xr", vxr), (VC, v)],
                   lambda e, acc=acc: e.tensor_add(acc, acc, b_xr[:, :]), inc=VC)
        v_acc.append(v)
    v_acc_all = r.val(VC)

    ow_free = [0, 0]
    ows = ["owa", "owb"]
    att_done = [0] * 8
    att_mm = [0] * 8
    for dc in range(4):
        for k in range(16):
            bufi = k % 2
            vw = dma("scalar", ows[bufi], s_ow[:, bufi * 512:(bufi + 1) * 512],
                     outwT[k * 128:(k + 1) * 128, dc * 512:(dc + 1) * 512],
                     deps=[(TC, ow_free[bufi])] + (barF if (dc == 0 and k < 2) else []))
            for tt in range(NTT):
                deps = [(ows[bufi], vw), ("zg", v_zg[k])]
                if k == 0:
                    deps.append((VC, att_done[tt]))
                att_mm[tt] = r.emit("tensor", deps,
                                    lambda e, k=k, tt=tt, bufi=bufi: e.matmul(
                                        p_att[tt][:, :],
                                        zA[:, k * FD + tt * 128:k * FD + (tt + 1) * 128],
                                        s_ow[:, bufi * 512:(bufi + 1) * 512],
                                        start=(k == 0), stop=(k == 15)), inc=TC)
            ow_free[bufi] = att_mm[-1]
        for tt in range(NTT):
            acc = b_acc[:, tt * 2048 + dc * 512:tt * 2048 + (dc + 1) * 512]
            v = r.emit("vector", [(TC, att_mm[tt]), (VC, v_acc_all)],
                       lambda e, acc=acc, tt=tt: e.tensor_add(acc, acc,
                                                              p_att[tt][:, :]), inc=VC)
            att_done[tt] = v
            if dc == 3:
                dma("sync", "od", out_tok[tt * 128:(tt + 1) * 128, :],
                    b_acc[:, tt * 2048:(tt + 1) * 2048], deps=[(VC, v)])
    r.wait_only("gpsimd", [("od", r.val("od"))])
    fin_scope.close()

    with nc.Block() as block:
        @block.sync
        def _(eng):
            r.replay("sync", eng)

        @block.tensor
        def _(eng):
            r.replay("tensor", eng)

        @block.vector
        def _(eng):
            r.replay("vector", eng)

        @block.scalar
        def _(eng):
            r.replay("scalar", eng)

        @block.gpsimd
        def _(eng):
            r.replay("gpsimd", eng)

    stack.close()
    return nc


def kernel(L0_attn_x, L0_attn_norm_w, L0_attn_norm_b, L0_attn_x_proj_w,
           L0_attn_dt_proj_w, L0_attn_dt_proj_b, L0_attn_A_log, L0_attn_D_param,
           L0_attn_out_proj_w, L0_ffn_router_w, L0_ffn_w1, L0_ffn_w2):
    x = np.ascontiguousarray(np.asarray(L0_attn_x, dtype=np.float32).reshape(L, D))
    xT = np.ascontiguousarray(x.T)
    xpwT_h = np.ascontiguousarray(np.asarray(L0_attn_x_proj_w, np.float32).T)
    routT_h = np.ascontiguousarray(np.asarray(L0_ffn_router_w, np.float32).T)
    dtw = np.asarray(L0_attn_dt_proj_w, np.float32)
    dtb = np.asarray(L0_attn_dt_proj_b, np.float32)
    alog = np.asarray(L0_attn_A_log, np.float32)
    dpar = np.asarray(L0_attn_D_param, np.float32)
    nw = np.asarray(L0_attn_norm_w, np.float32)
    nb = np.asarray(L0_attn_norm_b, np.float32)
    outwT_h = np.ascontiguousarray(np.asarray(L0_attn_out_proj_w, np.float32).T)
    w1 = np.asarray(L0_ffn_w1, np.float32)
    w2 = np.asarray(L0_ffn_w2, np.float32)

    w1r = w1.reshape(E, 2, 11, 128, 16, 128)
    w1q_h = np.ascontiguousarray(w1r.transpose(0, 2, 5, 1, 4, 3))
    w2r = w2.reshape(E, 4, 512, 11, 128)
    w2q_h = np.ascontiguousarray(w2r.transpose(0, 1, 3, 4, 2))

    ident_h = np.eye(128, dtype=np.float32)
    triu_h = np.triu(np.ones((128, 128), np.float32))
    ones_h = np.ones((128, 128), np.float32)
    ceoff_h = np.ascontiguousarray(np.broadcast_to(
        (np.arange(E, dtype=np.float32) * CAP - 1.0)[None, :], (128, E)))
    tokid_h = np.ascontiguousarray(np.arange(TLOC, dtype=np.int32).reshape(NTT, 128).T)
    nwk_h = np.ascontiguousarray(nw.reshape(16, 128).T)
    nbk_h = np.ascontiguousarray(nb.reshape(16, 128).T)

    nc = build_program()

    in_maps = []
    d_ids = np.arange(D, dtype=np.int32)
    for i in range(NCORES):
        dsl = slice(i * DSL, (i + 1) * DSL)
        tsl = slice(i * TLOC, (i + 1) * TLOC)
        zw_rows = (d_ids // DSL) * (DSL * NCORES) + (d_ids % DSL) * NCORES + i
        zw = np.ascontiguousarray(zw_rows.astype(np.int32).reshape(16, 128).T)
        in_maps.append({
            "x_tok": np.ascontiguousarray(x[tsl, :]),
            "x_dsl": np.ascontiguousarray(xT[dsl, :]),
            "xpwT": xpwT_h,
            "routT": routT_h,
            "dtwT": np.ascontiguousarray(dtw.T[:, dsl]),
            "dtb2": np.ascontiguousarray(dtb[dsl].reshape(2, 128).T),
            "alog2": np.ascontiguousarray(
                alog[dsl].reshape(2, 128, NST).transpose(1, 0, 2).reshape(128, 2 * NST)),
            "dcol2": np.ascontiguousarray(dpar[dsl].reshape(2, 128).T),
            "nwsl": np.ascontiguousarray(nw[dsl].reshape(2, 128).T),
            "nbsl": np.ascontiguousarray(nb[dsl].reshape(2, 128).T),
            "nwk": nwk_h,
            "nbk": nbk_h,
            "outwT": outwT_h,
            "w1q": w1q_h,
            "w2q": w2q_h,
            "ident": ident_h,
            "triuT": triu_h,
            "onesq": ones_h,
            "ceoff": ceoff_h,
            "tokid": tokid_h,
            "zwidx": zw,
        })

    res = run_bass_kernel_spmd(nc, in_maps, core_ids=list(range(NCORES)))
    out = np.concatenate([res.results[i]["out_tok"] for i in range(NCORES)], axis=0)
    return out.reshape(1, L, D).astype(np.float32)


# version 15
# speedup vs baseline: 1.1376x; 1.0920x over previous
import numpy as np
import contextlib
import concourse.bass as bass
import concourse.mybir as mybir
from concourse.bass_utils import run_bass_kernel_spmd

F32 = mybir.dt.float32
I32 = mybir.dt.int32
AF = mybir.ActivationFunctionType
ALU = mybir.AluOpType
AX = mybir.AxisListType

NCORES = 8
L = 8192
D = 2048
NST = 64
RNK = 128
PER = 512
TLOC = L // NCORES
NTT = TLOC // 128
DSL = D // NCORES
E = 8
IFF = 1408
CAP = 384
TRASH = E * CAP
NSLOT = E * CAP + 128
FD = 1024
NCH = L // FD
PRJ = RNK + 2 * NST
BROW = 260
EPS = 1e-5


class Rec:
    ENGS = ("sync", "tensor", "vector", "scalar", "gpsimd")

    def __init__(self, nc, stack):
        self.nc = nc
        self.stack = stack
        self.q = {e: [] for e in self.ENGS}
        self.semh = {}
        self.semv = {}
        self.lastw = {e: {} for e in self.ENGS}
        for s in ("tc", "vc", "sc", "gc", "od"):
            self.ensure(s)

    def ensure(self, s):
        if s not in self.semh:
            self.semh[s] = self.stack.enter_context(self.nc.semaphore(s))
            self.semv[s] = 0

    def emit(self, eng, deps, build, inc, inck=1):
        self.ensure(inc)
        for s, _v in deps:
            self.ensure(s)
        deps2 = []
        for s, v in deps:
            if v <= 0:
                continue
            if self.lastw[eng].get(s, 0) >= v:
                continue
            self.lastw[eng][s] = v
            deps2.append((s, v))
        self.semv[inc] += inck
        self.q[eng].append((deps2, build, inc, inck))
        return self.semv[inc]

    def val(self, sem):
        return self.semv[sem]

    def snap(self):
        return dict(self.semv)

    def bar(self, snapshot, skip=()):
        return [(s, v) for s, v in snapshot.items() if s not in skip]

    def wait_only(self, eng, deps):
        for s, _v in deps:
            self.ensure(s)
        deps2 = []
        for s, v in deps:
            if v <= 0 or self.lastw[eng].get(s, 0) >= v:
                continue
            self.lastw[eng][s] = v
            deps2.append((s, v))
        if deps2:
            self.q[eng].append((deps2, None, None, 0))

    def replay(self, eng_name, eng):
        for deps, build, incs, inck in self.q[eng_name]:
            for s, v in deps:
                eng.wait_ge(self.semh[s], v)
            if build is not None:
                build(eng).then_inc(self.semh[incs], inck)


def build_program():
    nc = bass.Bass("TRN2", target_bir_lowering=False)
    stack = contextlib.ExitStack()

    def din(name, shape, dtype=F32):
        return nc.dram_tensor(name, list(shape), dtype, kind="ExternalInput").ap()

    x_tok = din("x_tok", (TLOC, D))
    x_dsl = din("x_dsl", (DSL, L))
    xpwT = din("xpwT", (D, PRJ))
    routT = din("routT", (D, E))
    dtwT = din("dtwT", (RNK, DSL))
    dtb2 = din("dtb2", (128, 2))
    alog2 = din("alog2", (128, 2 * NST))
    dcol2 = din("dcol2", (128, 2))
    nwsl = din("nwsl", (128, 2))
    nbsl = din("nbsl", (128, 2))
    nwk = din("nwk", (128, 16))
    nbk = din("nbk", (128, 16))
    outwT = din("outwT", (D, D))
    w1q = din("w1q", (E, 11, 128, 2, 16, 128))
    w2q = din("w2q", (E, 4, 11, 128, 512))
    ident = din("ident", (128, 128))
    triuT = din("triuT", (128, 128))
    onesq = din("onesq", (128, 128))
    ceoff = din("ceoff", (128, E))
    tokid = din("tokid", (128, NTT), I32)
    zwidx = din("zwidx", (128, 16), I32)

    out_tok = nc.dram_tensor("out_tok", [TLOC, D], F32, kind="ExternalOutput").ap()

    projb = nc.dram_tensor("projb", [BROW, TLOC], F32).ap()
    pag = nc.dram_tensor("pag", [NCORES * BROW, TLOC], F32).ap()
    zb = nc.dram_tensor("zb", [DSL, L], F32).ap()
    zag = nc.dram_tensor("zag", [D * NCORES, FD], F32).ap()
    xnb = nc.dram_tensor("xnb", [TLOC, D], F32).ap()
    toksD = nc.dram_tensor("toksD", [NSLOT, 1], I32).ap()
    y2w = nc.dram_tensor("y2w", [NSLOT, D], F32).ap()
    flsh = nc.dram_tensor("flsh", [4, 128], F32).ap()

    r = Rec(nc, stack)
    TC, VC, SC, GC, OD = "tc", "vc", "sc", "gc", "od"

    def dma(eng, sem, out, in_, deps=()):
        return r.emit(eng, deps, lambda e, o=out, i=in_: e.dma_start(out=o, in_=i),
                      inc=sem, inck=16)

    def sbuf(name, shape, dtype=F32, scope=None):
        return (scope or stack).enter_context(nc.sbuf_tensor(name, list(shape), dtype))

    def psum(name, scope):
        return scope.enter_context(nc.psum_tensor(name, [128, 512], F32))

    c_id = sbuf("c_id", (128, 128))
    c_triu = sbuf("c_triu", (128, 128))
    c_ones = sbuf("c_ones", (128, 128))
    c_ceoff = sbuf("c_ceoff", (128, E))
    c_tokid = sbuf("c_tokid", (128, NTT), I32)
    c_zwidx = sbuf("c_zwidx", (128, 16), I32)
    s_xpw = sbuf("s_xpw", (128, 16 * PRJ))
    s_rout = sbuf("s_rout", (128, 16 * E))
    s_dtw = sbuf("s_dtw", (128, DSL))
    s_dtb = sbuf("s_dtb", (128, 2))
    s_alog = sbuf("s_alog", (128, 2 * NST))
    s_A = sbuf("s_A", (128, 2 * NST))
    s_dcol = sbuf("s_dcol", (128, 2))
    s_nwsl = sbuf("s_nwsl", (128, 2))
    s_nbsl = sbuf("s_nbsl", (128, 2))
    s_nwk = sbuf("s_nwk", (128, 16))
    s_nbk = sbuf("s_nbk", (128, 16))
    s_strow = sbuf("s_strow", (4, TLOC))
    s_lg = sbuf("s_lg", (128, NTT * E))
    s_w12 = sbuf("s_w12", (128, 2 * NTT))
    s_msk = sbuf("s_msk", (128, NTT * E))
    s_m1 = sbuf("s_m1", (128, NTT * E))
    s_m2 = sbuf("s_m2", (128, NTT * E))
    s_sloti = sbuf("s_sloti", (128, 2 * NTT), I32)
    s_stoks = sbuf("s_stoks", (128, NSLOT // 128), I32)
    s_zero = sbuf("s_zero", (128, 2048))
    s_col = sbuf("s_col", (128, 16))
    s_e8 = sbuf("s_e8", (128, 4 * E))

    dma("sync", "cst", c_id[:, :], ident)
    dma("sync", "cst", c_triu[:, :], triuT)
    dma("sync", "cst", c_ones[:, :], onesq)
    dma("sync", "cst", c_ceoff[:, :], ceoff)
    dma("sync", "cst", c_tokid[:, :], tokid)
    dma("sync", "cst", c_zwidx[:, :], zwidx)
    for k in range(16):
        dma("sync", "cst", s_xpw[:, k * PRJ:(k + 1) * PRJ], xpwT[k * 128:(k + 1) * 128, :])
        dma("sync", "cst", s_rout[:, k * E:(k + 1) * E], routT[k * 128:(k + 1) * 128, :])
    dma("sync", "cst", s_dtw[:, :], dtwT)
    dma("sync", "cst", s_dtb[:, :], dtb2)
    dma("sync", "cst", s_alog[:, :], alog2)
    dma("sync", "cst", s_dcol[:, :], dcol2)
    dma("sync", "cst", s_nwsl[:, :], nwsl)
    dma("sync", "cst", s_nbsl[:, :], nbsl)
    dma("sync", "cst", s_nwk[:, :], nwk)
    dma("sync", "cst", s_nbk[:, :], nbk)
    v_consts = r.val("cst")

    vs = r.emit("scalar", [("cst", v_consts)],
                lambda e: e.activation(s_A[:, :], s_alog[:, :], AF.Exp), inc=SC)
    vA = r.emit("vector", [(SC, vs)],
                lambda e: e.tensor_scalar_mul(s_A[:, :], s_A[:, :], -1.0), inc=VC)
    vz = r.emit("vector", [], lambda e: e.memset(s_zero[:, :], 0.0), inc=VC)
    dma("scalar", "tz", y2w[TRASH:TRASH + 128, :], s_zero[:, :], deps=[(VC, vz)])
    for i in range(NSLOT // 128):
        dma("scalar", "tz", toksD[i * 128:(i + 1) * 128, :],
            s_zero[:, i:i + 1].bitcast(I32), deps=[(VC, vz)])
    v_toks_zero = r.val("tz")

    tok_scope = contextlib.ExitStack()
    A = sbuf("A", (128, 12544), scope=tok_scope)
    a_xt = A[:, 0:2048]
    a_sq = A[:, 2048:4096]
    a_xn = A[:, 4096:6144]
    a_x2 = A[:, 6144:8192]
    a_xnT = A[:, 8192:10240]
    a_x2T = A[:, 10240:12288]
    a_prj = A[:, 12288:12288 + 256]
    p_t1 = psum("p_t1", tok_scope)
    p_t2 = psum("p_t2", tok_scope)
    p_pr = psum("p_pr", tok_scope)
    p_lg = psum("p_lg", tok_scope)
    p_st = psum("p_st", tok_scope)
    p_rk = psum("p_rk", tok_scope)

    c0, c1, c2, c3 = s_col[:, 0:1], s_col[:, 1:2], s_col[:, 2:3], s_col[:, 3:4]
    mu1, rs1, mu2, rs2 = s_col[:, 4:5], s_col[:, 5:6], s_col[:, 6:7], s_col[:, 7:8]
    stc = s_col[:, 8:12]

    tc_e = sc_e = vc_e = gd_e = 0
    pt_free = [0, 0]
    pr_free = 0
    lg_free = 0
    prj_dma = 0
    v_xn_dma = []
    v_strow = 0
    v_logit = []

    def ln_stats(src, mu, rs, extra_v, extra_s):
        v = r.emit("vector", extra_v, lambda e, src=src: e.reduce_sum(c0, src, axis=AX.X),
                   inc=VC)
        s = r.emit("scalar", extra_s, lambda e, src=src: e.activation(
            a_sq, src, AF.Square), inc=SC)
        v = r.emit("vector", [(SC, s), (VC, v)],
                   lambda e: e.reduce_sum(c1, a_sq, axis=AX.X), inc=VC)
        v = r.emit("vector", [(VC, v)],
                   lambda e, mu=mu: e.tensor_scalar_mul(mu, c0, 1.0 / D), inc=VC)
        v = r.emit("vector", [(VC, v)],
                   lambda e: e.tensor_scalar_mul(c2, c1, 1.0 / D), inc=VC)
        v = r.emit("vector", [(VC, v)],
                   lambda e, mu=mu: e.tensor_mul(c3, mu, mu), inc=VC)
        v = r.emit("vector", [(VC, v)],
                   lambda e: e.tensor_sub(c3, c2, c3), inc=VC)
        v = r.emit("vector", [(VC, v)],
                   lambda e: e.tensor_scalar_add(c3, c3, EPS), inc=VC)
        s = r.emit("scalar", [(VC, v)],
                   lambda e: e.activation(c3, c3, AF.Sqrt), inc=SC)
        v = r.emit("vector", [(SC, s)],
                   lambda e, rs=rs: e.reciprocal(rs, c3), inc=VC)
        return v

    for tt in range(NTT):
        vxd = dma("sync", "xl", a_xt, x_tok[tt * 128:(tt + 1) * 128, :],
                  deps=[(VC, vc_e), (SC, sc_e)])
        v = ln_stats(a_xt, mu1, rs1, [("xl", vxd)], [("xl", vxd), (VC, vc_e)])
        v_xn = r.emit("vector", [(VC, v), (TC, tc_e), ("xnb", gd_e)],
                      lambda e: e.tensor_scalar(a_xn, a_xt, mu1, rs1,
                                                ALU.subtract, ALU.mult), inc=VC)
        v = ln_stats(a_xn, mu2, rs2, [(VC, v_xn)], [(VC, v_xn)])
        v_x2 = r.emit("vector", [(VC, v), (TC, tc_e)],
                      lambda e: e.tensor_scalar(a_x2, a_xn, mu2, rs2,
                                                ALU.subtract, ALU.mult), inc=VC)
        v_xn_dma.append(dma("gpsimd", "xnb", xnb[tt * 128:(tt + 1) * 128, :], a_xn,
                            deps=[(VC, v_xn)]))
        va = r.emit("vector", [(VC, v_x2)],
                    lambda e: e.tensor_copy(s_col[:, 8:9], rs1), inc=VC)
        va = r.emit("vector", [(VC, va)],
                    lambda e: e.tensor_mul(s_col[:, 9:10], mu1, rs1), inc=VC)
        va = r.emit("vector", [(VC, va)],
                    lambda e: e.tensor_copy(s_col[:, 10:11], rs2), inc=VC)
        va = r.emit("vector", [(VC, va)],
                    lambda e: e.tensor_mul(s_col[:, 11:12], mu2, rs2), inc=VC)
        vt = r.emit("tensor", [(VC, va), ("cst", v_consts), (SC, v_strow)],
                    lambda e: e.transpose(p_st[0:4, 0:128], stc, c_id[:, :]), inc=TC)
        v_strow = r.emit("scalar", [(TC, vt)],
                         lambda e, tt=tt: e.activation(
                             s_strow[0:4, tt * 128:(tt + 1) * 128],
                             p_st[0:4, 0:128], AF.Copy), inc=SC)
        v_evac = 0
        for src, dst, scaled, vdep in ((a_xn, a_xnT, False, v_xn),
                                       (a_x2, a_x2T, True, v_x2)):
            for k in range(16):
                pi = k % 2
                pp = p_t1 if pi == 0 else p_t2
                vt = r.emit("tensor", [(VC, vdep), (SC, pt_free[pi])],
                            lambda e, pp=pp, src=src, k=k: e.transpose(
                                pp[:, 0:128], src[:, k * 128:(k + 1) * 128],
                                c_id[:, :]), inc=TC)
                v_evac = r.emit(
                    "scalar", [(TC, vt), ("cst", v_consts)],
                    (lambda e, pp=pp, dst=dst, k=k: e.activation(
                        dst[:, k * 128:(k + 1) * 128], pp[:, 0:128], AF.Identity,
                        bias=s_nbk[:, k:k + 1], scale=s_nwk[:, k:k + 1]))
                    if scaled else
                    (lambda e, pp=pp, dst=dst, k=k: e.activation(
                        dst[:, k * 128:(k + 1) * 128], pp[:, 0:128], AF.Copy)),
                    inc=SC)
                pt_free[pi] = v_evac
        for pt in range(2):
            for k in range(16):
                vt = r.emit("tensor",
                            [(SC, v_evac)] + ([(SC, pr_free)] if k == 0 else []),
                            lambda e, pt=pt, k=k: e.matmul(
                                p_pr[:, 0:128],
                                s_xpw[:, k * PRJ + pt * 128:k * PRJ + (pt + 1) * 128],
                                a_x2T[:, k * 128:(k + 1) * 128],
                                start=(k == 0), stop=(k == 15)), inc=TC)
            pr_free = r.emit("scalar", [(TC, vt), ("prj", prj_dma)],
                             lambda e, pt=pt: e.activation(
                                 a_prj[:, 0:128], p_pr[:, 0:128], AF.Copy), inc=SC)
            prj_dma = dma("gpsimd", "prj",
                          projb[pt * 128:(pt + 1) * 128, tt * 128:(tt + 1) * 128],
                          a_prj[:, 0:128], deps=[(SC, pr_free)])
        for k in range(16):
            vt = r.emit("tensor",
                        [(SC, v_evac)] + ([(SC, lg_free)] if k == 0 else []),
                        lambda e, k=k: e.matmul(
                            p_lg[:, 0:E], a_xnT[:, k * 128:(k + 1) * 128],
                            s_rout[:, k * E:(k + 1) * E],
                            start=(k == 0), stop=(k == 15)), inc=TC)
        lg_free = r.emit("scalar", [(TC, vt)],
                         lambda e, tt=tt: e.activation(
                             s_lg[:, tt * E:(tt + 1) * E], p_lg[:, 0:E], AF.Copy),
                         inc=SC)
        v_logit.append(lg_free)
        tc_e, sc_e, vc_e, gd_e = r.val(TC), r.val(SC), r.val(VC), r.val("xnb")

    e0, e1t, e2t, e3t = (s_e8[:, 0:E], s_e8[:, E:2 * E], s_e8[:, 2 * E:3 * E],
                         s_e8[:, 3 * E:4 * E])

    def cumsum8(src, t1, t2, dep):
        v = r.emit("vector", [(VC, dep)], lambda e: e.tensor_copy(t1, src), inc=VC)
        v = r.emit("vector", [(VC, v)],
                   lambda e: e.tensor_add(t1[:, 1:8], src[:, 1:8], src[:, 0:7]), inc=VC)
        v = r.emit("vector", [(VC, v)], lambda e: e.tensor_copy(t2, t1), inc=VC)
        v = r.emit("vector", [(VC, v)],
                   lambda e: e.tensor_add(t2[:, 2:8], t1[:, 2:8], t1[:, 0:6]), inc=VC)
        v = r.emit("vector", [(VC, v)], lambda e: e.tensor_copy(t1, t2), inc=VC)
        v = r.emit("vector", [(VC, v)],
                   lambda e: e.tensor_add(t1[:, 4:8], t2[:, 4:8], t2[:, 0:4]), inc=VC)
        return t1, v

    for tt in range(NTT):
        lg = s_lg[:, tt * E:(tt + 1) * E]
        m1 = s_m1[:, tt * E:(tt + 1) * E]
        m2 = s_m2[:, tt * E:(tt + 1) * E]
        v = r.emit("vector", [(SC, v_logit[tt])],
                   lambda e, lg=lg: e.reduce_max(c0, lg, axis=AX.X), inc=VC)
        v = r.emit("vector", [(VC, v)],
                   lambda e, lg=lg: e.tensor_single_scalar(e0, lg, c0, ALU.is_equal),
                   inc=VC)
        cs, v = cumsum8(e0, e1t, e2t, v)
        v = r.emit("vector", [(VC, v)],
                   lambda e, cs=cs: e.tensor_single_scalar(e2t, cs, 1.0, ALU.is_equal),
                   inc=VC)
        v = r.emit("vector", [(VC, v)], lambda e, m1=m1: e.tensor_mul(m1, e0, e2t),
                   inc=VC)
        v = r.emit("vector", [(VC, v)],
                   lambda e, m1=m1: e.tensor_scalar_mul(e0, m1, 1e30), inc=VC)
        v = r.emit("vector", [(VC, v)], lambda e, lg=lg: e.tensor_sub(e0, lg, e0),
                   inc=VC)
        v = r.emit("vector", [(VC, v)], lambda e: e.reduce_max(c1, e0, axis=AX.X),
                   inc=VC)
        v = r.emit("vector", [(VC, v)],
                   lambda e: e.tensor_single_scalar(e1t, e0, c1, ALU.is_equal), inc=VC)
        cs2, v = cumsum8(e1t, e2t, e3t, v)
        v = r.emit("vector", [(VC, v)],
                   lambda e, cs2=cs2: e.tensor_single_scalar(e3t, cs2, 1.0,
                                                             ALU.is_equal), inc=VC)
        v = r.emit("vector", [(VC, v)], lambda e, m2=m2: e.tensor_mul(m2, e1t, e3t),
                   inc=VC)
        v = r.emit("vector", [(VC, v)], lambda e: e.tensor_sub(c2, c1, c0), inc=VC)
        s = r.emit("scalar", [(VC, v)], lambda e: e.activation(c3, c2, AF.Exp), inc=SC)
        v = r.emit("vector", [(SC, s)], lambda e: e.tensor_scalar_add(c2, c3, 1.0),
                   inc=VC)
        v = r.emit("vector", [(VC, v)],
                   lambda e, tt=tt: e.reciprocal(s_w12[:, 2 * tt:2 * tt + 1], c2),
                   inc=VC)
        v = r.emit("vector", [(VC, v)],
                   lambda e, tt=tt: e.tensor_mul(s_w12[:, 2 * tt + 1:2 * tt + 2], c3,
                                                 s_w12[:, 2 * tt:2 * tt + 1]), inc=VC)
        v = r.emit("vector", [(VC, v)],
                   lambda e, m1=m1, m2=m2, tt=tt: e.tensor_add(
                       s_msk[:, tt * E:(tt + 1) * E], m1, m2), inc=VC)
    v_masks = r.val(VC)

    rk_free = 0
    v_slots = 0
    for tt in range(NTT):
        for j in range(tt + 1):
            lhs = c_ones if j < tt else c_triu
            deps = [(VC, v_masks), ("cst", v_consts)]
            if j == 0:
                deps.append((SC, rk_free))
            vt = r.emit("tensor", deps,
                        lambda e, j=j, tt=tt, lhs=lhs: e.matmul(
                            p_rk[:, 0:E], lhs[:, :], s_msk[:, j * E:(j + 1) * E],
                            start=(j == 0), stop=(j == tt)), inc=TC)
        rk_free = r.emit("scalar", [(TC, vt), (VC, v_masks)],
                         lambda e, tt=tt: e.activation(
                             s_lg[:, tt * E:(tt + 1) * E], p_rk[:, 0:E], AF.Copy),
                         inc=SC)
        rk = s_lg[:, tt * E:(tt + 1) * E]
        v = r.emit("vector", [(SC, rk_free), ("cst", v_consts)],
                   lambda e, rk=rk: e.tensor_add(e0, rk, c_ceoff[:, :]), inc=VC)
        v = r.emit("vector", [(VC, v)],
                   lambda e, rk=rk: e.tensor_single_scalar(
                       e1t, rk, float(CAP) + 0.5, ALU.is_gt), inc=VC)
        v = r.emit("vector", [(VC, v)],
                   lambda e: e.tensor_scalar(e2t, e0, -1.0, float(TRASH),
                                             ALU.mult, ALU.add), inc=VC)
        v = r.emit("vector", [(VC, v)], lambda e: e.tensor_mul(e2t, e2t, e1t), inc=VC)
        v = r.emit("vector", [(VC, v)], lambda e: e.tensor_add(e0, e0, e2t), inc=VC)
        v = r.emit("vector", [(VC, v)],
                   lambda e, tt=tt: e.tensor_mul(e2t, s_m1[:, tt * E:(tt + 1) * E], e0),
                   inc=VC)
        v = r.emit("vector", [(VC, v)],
                   lambda e: e.reduce_sum(c0, e2t, axis=AX.X), inc=VC)
        v = r.emit("vector", [(VC, v)],
                   lambda e, tt=tt: e.tensor_mul(e2t, s_m2[:, tt * E:(tt + 1) * E], e0),
                   inc=VC)
        v = r.emit("vector", [(VC, v)],
                   lambda e: e.reduce_sum(c1, e2t, axis=AX.X), inc=VC)
        v = r.emit("vector", [(VC, v)],
                   lambda e, tt=tt: e.tensor_copy(s_sloti[:, 2 * tt:2 * tt + 1], c0),
                   inc=VC)
        v_slots = r.emit("vector", [(VC, v)],
                         lambda e, tt=tt: e.tensor_copy(
                             s_sloti[:, 2 * tt + 1:2 * tt + 2], c1), inc=VC)

    for tt in range(NTT):
        for si in range(2):
            r.emit("gpsimd", [(VC, v_slots), ("tz", v_toks_zero), ("cst", v_consts)],
                   lambda e, tt=tt, si=si: e.indirect_dma_start(
                       out=toksD, out_offset=bass.IndirectOffsetOnAxis(
                           ap=s_sloti[:, 2 * tt + si:2 * tt + si + 1], axis=0),
                       in_=c_tokid[:, tt:tt + 1], in_offset=None),
                   inc="sca", inck=16)
    v_scat = dma("gpsimd", "sca", flsh[0:1, 0:1], toksD[0:1, 0:1].bitcast(F32),
                 deps=[("sca", r.val("sca"))])
    for cc in range(NSLOT // 128):
        vtk = dma("sync", "stk", s_stoks[:, cc:cc + 1],
                  toksD[cc * 128:(cc + 1) * 128, :], deps=[("sca", v_scat)])

    dma("gpsimd", "prj", projb[256:260, :], s_strow[:, :], deps=[(SC, v_strow)])
    v_projb = r.val("prj")

    r.wait_only("gpsimd", [("prj", v_projb)])
    r.emit("gpsimd", [],
           lambda e: e.collective_compute(
               "AllGather", ALU.bypass, replica_groups=[list(range(NCORES))],
               ins=[projb.opt()], outs=[pag.opt()]), inc=GC)
    v_ag = r.val(GC)
    tok_scope.close()
    tok_snap = r.snap()
    barT = r.bar(tok_snap, skip=("od",))

    moe_scope = contextlib.ExitStack()
    b_xgtm = sbuf("b_xgtm", (128, 3 * 2048), scope=moe_scope)
    b_xgfm = sbuf("b_xgfm", (128, 16 * CAP), scope=moe_scope)
    s_yact = sbuf("s_yact", (128, 11 * CAP), scope=moe_scope)
    s_w1 = sbuf("s_w1", (128, 3 * 4096), scope=moe_scope)
    s_w2 = sbuf("s_w2", (128, 4 * 512), scope=moe_scope)
    s_sg = sbuf("s_sg", (128, CAP), scope=moe_scope)
    s_y2s = sbuf("s_y2s", (128, 2 * 512), scope=moe_scope)
    p_xg = psum("p_xg", moe_scope)
    p_g = psum("p_g", moe_scope)
    p_u = psum("p_u", moe_scope)
    p_m2 = [psum(f"p_m2_{m}", moe_scope) for m in range(3)]

    w1_free = [0, 0, 0]
    w2_free = [0, 0, 0, 0]
    w1s = ["w1a", "w1b", "w1c"]
    w2s = ["w2a", "w2b", "w2c", "w2d"]
    y2s = ["y2a", "y2b"]
    y2s_free = [0, 0]
    pg_free = 0
    pu_free = 0
    pm_free = [0, 0, 0]
    xg_free = 0
    v_xn_all = v_xn_dma[-1]
    v_yact = 0
    v_moe_y2 = 0

    for ee in range(E):
        v_g = []
        for mc in range(3):
            v_g.append(r.emit(
                "gpsimd", [("stk", vtk), ("xnb", v_xn_all), (TC, xg_free)],
                lambda e, ee=ee, mc=mc: e.indirect_dma_start(
                    out=b_xgtm[:, mc * 2048:(mc + 1) * 2048], out_offset=None,
                    in_=xnb, in_offset=bass.IndirectOffsetOnAxis(
                        ap=s_stoks[:, ee * 3 + mc:ee * 3 + mc + 1], axis=0)),
                inc="mg", inck=16))
        v_gf = dma("gpsimd", "mg", flsh[1:2, :], b_xgtm[0:1, 0:128],
                   deps=[("mg", v_g[-1])])
        v_g = [v_gf, v_gf, v_gf]
        v_tr = 0
        for k in range(16):
            for mc in range(3):
                vt = r.emit("tensor", [("mg", v_g[mc]), ("cst", v_consts), (SC, v_tr)],
                            lambda e, k=k, mc=mc: e.transpose(
                                p_xg[:, 0:128],
                                b_xgtm[:, mc * 2048 + k * 128:mc * 2048 + (k + 1) * 128],
                                c_id[:, :]), inc=TC)
                v_tr = r.emit("scalar", [(TC, vt)],
                              lambda e, k=k, mc=mc: e.activation(
                                  b_xgfm[:, k * CAP + mc * 128:k * CAP + (mc + 1) * 128],
                                  p_xg[:, 0:128], AF.Copy), inc=SC)
        xg_free = r.val(TC)
        for iu in range(11):
            bufi = iu % 3
            vw = dma("scalar", w1s[bufi], s_w1[:, bufi * 4096:(bufi + 1) * 4096],
                     w1q[ee, iu].rearrange("dd j k ii -> dd (j k ii)"),
                     deps=[(TC, w1_free[bufi])] +
                     (barT if (ee == 0 and iu < 3) else []))
            for j, psm, free in ((0, p_g, [(SC, pg_free)]), (1, p_u, [(VC, pu_free)])):
                for k in range(16):
                    deps = [(w1s[bufi], vw), (SC, v_tr)]
                    if k == 0:
                        deps += free
                    vt = r.emit("tensor", deps,
                                lambda e, j=j, k=k, bufi=bufi, psm=psm: e.matmul(
                                    psm[:, 0:CAP],
                                    s_w1[:, bufi * 4096 + (j * 16 + k) * 128:
                                         bufi * 4096 + (j * 16 + k + 1) * 128],
                                    b_xgfm[:, k * CAP:(k + 1) * CAP],
                                    start=(k == 0), stop=(k == 15)), inc=TC)
                w1_free[bufi] = vt
            pg_free = r.emit("scalar", [(TC, w1_free[bufi]), (VC, v_yact)],
                             lambda e: e.activation(s_sg[:, :], p_g[:, 0:CAP],
                                                    AF.Silu), inc=SC)
            v_yact = r.emit("vector", [(SC, pg_free)],
                            lambda e, iu=iu: e.tensor_mul(
                                s_yact[:, iu * CAP:(iu + 1) * CAP], s_sg[:, :],
                                p_u[:, 0:CAP]), inc=VC)
            pu_free = v_yact
        for dc in range(4):
            for k in range(11):
                bufi = k % 4
                vw2 = dma("scalar", w2s[bufi], s_w2[:, bufi * 512:(bufi + 1) * 512],
                          w2q[ee, dc, k],
                          deps=[(TC, w2_free[bufi])] +
                          (barT if (ee == 0 and dc == 0 and k < 4) else []))
                for m in range(3):
                    deps = [(w2s[bufi], vw2), (VC, v_yact)]
                    if k == 0:
                        deps.append((SC, pm_free[m]))
                    vt = r.emit("tensor", deps,
                                lambda e, m=m, k=k, bufi=bufi, dc=dc: e.matmul(
                                    p_m2[m][:, :],
                                    s_yact[:, k * CAP + m * 128:k * CAP + (m + 1) * 128],
                                    s_w2[:, bufi * 512:(bufi + 1) * 512],
                                    start=(k == 0), stop=(k == 10)), inc=TC)
                w2_free[bufi] = vt
            for m in range(3):
                sbi = m % 2
                pm_free[m] = r.emit(
                    "scalar", [(TC, w2_free[(10) % 4]), (y2s[sbi], y2s_free[sbi])],
                    lambda e, m=m, sbi=sbi: e.activation(
                        s_y2s[:, sbi * 512:(sbi + 1) * 512], p_m2[m][:, :], AF.Copy),
                    inc=SC)
                y2s_free[sbi] = dma(
                    "scalar", y2s[sbi],
                    y2w[ee * CAP + m * 128:ee * CAP + (m + 1) * 128,
                        dc * 512:(dc + 1) * 512],
                    s_y2s[:, sbi * 512:(sbi + 1) * 512], deps=[(SC, pm_free[m])])
    moe_scope.close()
    moe_snap = r.snap()

    scan_scope = contextlib.ExitStack()
    s_dr = sbuf("s_dr", (128, FD), scope=scan_scope)
    s_stb = sbuf("s_stb", (128, 4 * FD), scope=scan_scope)
    s_xc = sbuf("s_xc", (128, FD), scope=scan_scope)
    s_t0 = sbuf("s_t0", (128, FD), scope=scan_scope)
    s_dl = sbuf("s_dl", (128, 2 * FD), scope=scan_scope)
    s_dx = sbuf("s_dx", (128, 2 * FD), scope=scan_scope)
    s_ysb = sbuf("s_ysb", (128, 2 * FD), scope=scan_scope)
    s_dA = sbuf("s_dA", (128, 3 * FD), scope=scan_scope)
    s_u = sbuf("s_u", (128, 3 * FD), scope=scan_scope)
    s_h = sbuf("s_h", (128, 3 * FD), scope=scan_scope)
    s_tmp = sbuf("s_tmp", (128, 3 * FD), scope=scan_scope)
    s_bbc = sbuf("s_bbc", (128, 4 * FD), scope=scan_scope)
    s_cbc = sbuf("s_cbc", (128, 3 * FD), scope=scan_scope)
    s_state = sbuf("s_state", (128, 2 * NST), scope=scan_scope)
    p_dt = psum("p_dt", scan_scope)
    p_y = [scan_scope.enter_context(nc.psum_tensor(f"p_y{g}", [128, FD], F32))
           for g in range(2)]

    SL = {}
    for nm, t in (("dl", s_dl), ("dx", s_dx), ("ysb", s_ysb)):
        SL[nm] = [t[:, 0:FD], t[:, FD:2 * FD]]
    RG = {}
    for nm, t, nr in (("dA", s_dA, 3), ("u", s_u, 3), ("h", s_h, 3),
                      ("tmp", s_tmp, 3), ("bbc", s_bbc, 4), ("cbc", s_cbc, 3)):
        RG[nm] = [t[:, i * FD:(i + 1) * FD] for i in range(nr)]
    stbs = [s_stb[:, j * FD:(j + 1) * FD] for j in range(4)]

    barQ = r.bar(moe_snap, skip=("od",))
    bbc_used = [0, 0, 0, 0]
    cbc_used = [0, 0, 0]
    dA_used = [0, 0, 0]
    u_used = [0, 0, 0]
    h_free_sc = [0, 0, 0]
    h_pool = [0, 0, 0]
    tmp_used = [0, 0, 0]
    y_evac = [0, 0]
    st_sc = [[0] * NST, [0] * NST]
    v_zb = [0, 0]
    zbs = ["z0", "z1"]
    bsem = ["b0", "b1", "b2", "b3"]
    csem = ["ca", "cb", "cc2"]
    stb_free = 0
    dr_free = 0
    xc_free = 0
    vy_prev = [0, 0]
    pdt_free = 0
    t0_sc = 0
    it = 0

    for c in range(NCH):
        blk = c * BROW
        first = barQ if c == 0 else []
        v_stb = 0
        for j in range(4):
            v_stb = dma("sync", "stb", stbs[j],
                        pag[blk + 256 + j:blk + 257 + j, :].to_broadcast([128, FD]),
                        deps=first + [(GC, v_ag), (VC, stb_free)])
        v_dr = dma("sync", "dr", s_dr[:, :], pag[blk:blk + 128, :],
                   deps=first + [(GC, v_ag), (TC, dr_free)])
        vdl = [0, 0]
        for g in range(2):
            vxc = dma("sync", "xc", s_xc[:, :],
                      x_dsl[g * 128:(g + 1) * 128, c * FD:(c + 1) * FD],
                      deps=first + [(VC, xc_free)])
            v = r.emit("vector", [("xc", vxc), ("stb", v_stb), (SC, vy_prev[g]),
                                  (SC, t0_sc)],
                       lambda e: e.tensor_mul(s_t0[:, :], s_xc[:, :], stbs[0]), inc=VC)
            xc_free = v
            v_xno = r.emit("vector", [(VC, v)],
                           lambda e: e.tensor_sub(s_t0[:, :], s_t0[:, :], stbs[1]),
                           inc=VC)
            vy = r.emit("scalar", [(VC, v_xno), ("cst", v_consts),
                                   (zbs[g], v_zb[g])],
                        lambda e, g=g: e.activation(SL["ysb"][g], s_t0[:, :], AF.Copy,
                                                    scale=s_dcol[:, g:g + 1]), inc=SC)
            vy_prev[g] = vy
            v = r.emit("vector", [(VC, v_xno)],
                       lambda e, g=g: e.tensor_mul(SL["dx"][g], s_t0[:, :], stbs[2]),
                       inc=VC)
            v = r.emit("vector", [(VC, v)],
                       lambda e, g=g: e.tensor_sub(SL["dx"][g], SL["dx"][g], stbs[3]),
                       inc=VC)
            vs2 = r.emit("scalar", [(VC, v), ("cst", v_consts)],
                         lambda e, g=g: e.activation(SL["dx"][g], SL["dx"][g], AF.Identity,
                                                     bias=s_nbsl[:, g:g + 1],
                                                     scale=s_nwsl[:, g:g + 1]), inc=SC)
            va = 0
            for f in range(2):
                vt = r.emit("tensor",
                            [("dr", v_dr), ("cst", v_consts),
                             (VC, va if f == 1 else pdt_free)],
                            lambda e, g=g, f=f: e.matmul(
                                p_dt[:, :], s_dtw[:, g * 128:(g + 1) * 128],
                                s_dr[:, f * 512:(f + 1) * 512],
                                start=True, stop=True), inc=TC)
                dr_free = vt
                va = r.emit("vector", [(TC, vt), ("cst", v_consts), (SC, vy)],
                            lambda e, f=f, g=g: e.tensor_scalar_add(
                                s_t0[:, f * 512:(f + 1) * 512], p_dt[:, :],
                                s_dtb[:, g:g + 1]), inc=VC)
            pdt_free = va
            v = r.emit("vector", [(VC, va)],
                       lambda e: e.tensor_scalar_mul(s_xc[:, :], s_t0[:, :], -1.0),
                       inc=VC)
            v = r.emit("vector", [(VC, v)],
                       lambda e: e.tensor_tensor(s_xc[:, :], s_xc[:, :], s_t0[:, :],
                                                 ALU.min), inc=VC)
            vs = r.emit("scalar", [(VC, v)],
                        lambda e: e.activation(s_xc[:, :], s_xc[:, :], AF.Exp), inc=SC)
            vs = r.emit("scalar", [(SC, vs)],
                        lambda e: e.activation(s_xc[:, :], s_xc[:, :], AF.Ln, bias=1.0),
                        inc=SC)
            vs = r.emit("scalar", [(SC, vs)],
                        lambda e, g=g: e.activation(SL["dl"][g], s_t0[:, :], AF.Relu),
                        inc=SC)
            t0_sc = vs
            v = r.emit("vector", [(SC, vs)],
                       lambda e, g=g: e.tensor_add(SL["dl"][g], SL["dl"][g],
                                                   s_xc[:, :]), inc=VC)
            xc_free = v
            vdl[g] = v
            r.emit("vector", [(VC, v), (SC, vs2)],
                   lambda e, g=g: e.tensor_mul(SL["dx"][g], SL["dx"][g], SL["dl"][g]),
                   inc=VC)
        v_prep = r.val(VC)
        stb_free = v_prep

        for n in range(NST):
            bi = (c * NST + n) % 4
            ci = (c * NST + n) % 3
            vb = dma("sync", bsem[bi], RG["bbc"][bi],
                     pag[blk + 128 + n:blk + 129 + n, :].to_broadcast([128, FD]),
                     deps=[(VC, bbc_used[bi]), (GC, v_ag)])
            vc1 = dma("sync", csem[ci], RG["cbc"][ci][:, 0:PER],
                      pag[192 + n:193 + n, 0:PER].to_broadcast([128, PER]),
                      deps=[(GC, cbc_used[ci]), (GC, v_ag)] + (barQ if it < 2 else []))
            vc1 = dma("sync", csem[ci], RG["cbc"][ci][:, PER:FD],
                      pag[192 + n:193 + n, 0:PER].to_broadcast([128, PER]),
                      deps=[(GC, v_ag)])
            for g in range(2):
                di = it % 3
                vsA = r.emit("scalar",
                             [(VC, max(vdl[g], dA_used[di])), (VC, vA)],
                             lambda e, g=g, n=n, di=di: e.activation(
                                 RG["dA"][di], SL["dl"][g], AF.Exp,
                                 scale=s_A[:, g * NST + n:g * NST + n + 1]), inc=SC)
                vu = r.emit("vector", [(bsem[bi], vb),
                                       (VC, max(u_used[di], v_prep))],
                            lambda e, g=g, di=di, bi=bi: e.tensor_mul(
                                RG["u"][di], SL["dx"][g], RG["bbc"][bi]), inc=VC)
                bbc_used[bi] = vu
                hi = it % 3
                init = 0.0 if c == 0 else s_state[:, g * NST + n:g * NST + n + 1]
                vh = r.emit("vector",
                            [(SC, vsA), (VC, vu), (SC, h_free_sc[hi]),
                             (GC, h_pool[hi])] +
                            ([(SC, st_sc[g][n])] if c > 0 else []),
                            lambda e, hi=hi, di=di, init=init: e.tensor_tensor_scan(
                                RG["h"][hi], RG["dA"][di], RG["u"][di], init,
                                ALU.mult, ALU.add), inc=VC)
                dA_used[di] = vh
                u_used[di] = vh
                st_sc[g][n] = r.emit("scalar", [(VC, vh)],
                                     lambda e, hi=hi, g=g, n=n: e.activation(
                                         s_state[:, g * NST + n:g * NST + n + 1],
                                         RG["h"][hi][:, FD - 1:FD], AF.Copy), inc=SC)
                h_free_sc[hi] = st_sc[g][n]
                ti = it % 3
                vtm = r.emit("gpsimd", [(csem[ci], vc1), (TC, tmp_used[ti]),
                                        (VC, vh)],
                             lambda e, hi=hi, ti=ti, ci=ci: e.tensor_mul(
                                 RG["tmp"][ti], RG["h"][hi], RG["cbc"][ci]), inc=GC)
                cbc_used[ci] = vtm
                h_pool[hi] = vtm
                if n == 0:
                    vt = r.emit("tensor", [(SC, vy_prev[g]), ("cst", v_consts)],
                                lambda e, g=g: e.matmul(
                                    p_y[g][:, 0:512], c_id[:, :],
                                    SL["ysb"][g][:, 0:512],
                                    start=True, stop=False), inc=TC)
                    vt = r.emit("tensor", [],
                                lambda e, g=g: e.matmul(
                                    p_y[g][:, 512:FD], c_id[:, :],
                                    SL["ysb"][g][:, 512:FD],
                                    start=True, stop=False), inc=TC)
                for half in range(2):
                    vt = r.emit("tensor", [(GC, vtm)],
                                lambda e, g=g, ti=ti, half=half, n=n: e.matmul(
                                    p_y[g][:, half * 512:(half + 1) * 512],
                                    c_id[:, :],
                                    RG["tmp"][ti][:, half * 512:(half + 1) * 512],
                                    start=False, stop=(n == NST - 1)), inc=TC)
                tmp_used[ti] = vt
                it += 1
        for g in range(2):
            ve = r.emit("scalar", [(TC, tmp_used[(it - 1) % 3]),
                                   (zbs[g], v_zb[g])],
                        lambda e, g=g: e.activation(SL["ysb"][g], p_y[g][:, :],
                                                    AF.Copy), inc=SC)
            y_evac[g] = ve
            v_zb[g] = dma("sync", zbs[g],
                          zb[g * 128:(g + 1) * 128, c * FD:(c + 1) * FD],
                          SL["ysb"][g], deps=[(SC, ve)])
    scan_scope.close()

    r.wait_only("gpsimd", [(zbs[0], v_zb[0]), (zbs[1], v_zb[1])])
    r.emit("gpsimd", [],
           lambda e: e.collective_compute(
               "AllGather", ALU.bypass, replica_groups=[list(range(NCORES))],
               ins=[zb.opt()], outs=[zag.opt()]), inc=GC)
    v_zag = r.val(GC)
    scan_snap = r.snap()
    barF = r.bar(scan_snap, skip=("od",))

    fin_scope = contextlib.ExitStack()
    b_acc = sbuf("b_acc", (128, NTT * 2048), scope=fin_scope)
    b_g1 = sbuf("b_g1", (128, 2048), scope=fin_scope)
    b_g2 = sbuf("b_g2", (128, 2048), scope=fin_scope)
    b_xr = sbuf("b_xr", (128, 2048), scope=fin_scope)
    zA = sbuf("zA", (128, 16 * FD), scope=fin_scope)
    s_ow = sbuf("s_ow", (128, 2 * 512), scope=fin_scope)
    p_att = [psum(f"p_att{i}", fin_scope) for i in range(8)]

    v_zg = []
    for k in range(16):
        v_zg.append(r.emit(
            "gpsimd", [(GC, v_zag), ("cst", v_consts)] + (barF if k == 0 else []),
            lambda e, k=k: e.indirect_dma_start(
                out=zA[:, k * FD:(k + 1) * FD], out_offset=None, in_=zag,
                in_offset=bass.IndirectOffsetOnAxis(ap=c_zwidx[:, k:k + 1], axis=0)),
            inc="zg", inck=16))
    v_zgf = dma("gpsimd", "zg", flsh[2:3, :], zA[0:1, 0:128], deps=[("zg", v_zg[15])])
    v_zg = [v_zgf] * 16

    v_acc = []
    for tt in range(NTT):
        vg1 = r.emit("gpsimd",
                     [(y2s[0], y2s_free[0]), (y2s[1], y2s_free[1]),
                      (VC, v_slots)] +
                     ([(VC, v_acc[-1])] if v_acc else barF),
                     lambda e, tt=tt: e.indirect_dma_start(
                         out=b_g1[:, :], out_offset=None, in_=y2w,
                         in_offset=bass.IndirectOffsetOnAxis(
                             ap=s_sloti[:, 2 * tt:2 * tt + 1], axis=0)),
                     inc="fg", inck=16)
        vg2 = r.emit("gpsimd", [],
                     lambda e, tt=tt: e.indirect_dma_start(
                         out=b_g2[:, :], out_offset=None, in_=y2w,
                         in_offset=bass.IndirectOffsetOnAxis(
                             ap=s_sloti[:, 2 * tt + 1:2 * tt + 2], axis=0)),
                     inc="fg", inck=16)
        vg2 = dma("gpsimd", "fg", flsh[3:4, :], b_g2[0:1, 0:128], deps=[("fg", vg2)])
        vg1 = vg2
        vxr = dma("sync", "xr", b_xr[:, :], x_tok[tt * 128:(tt + 1) * 128, :],
                  deps=([(VC, v_acc[-1])] if v_acc else barF))
        acc = b_acc[:, tt * 2048:(tt + 1) * 2048]
        v = r.emit("vector", [("fg", vg1)] + (barF if tt == 0 else []),
                   lambda e, acc=acc, tt=tt: e.tensor_scalar(
                       acc, b_g1[:, :], s_w12[:, 2 * tt:2 * tt + 1], None, ALU.mult),
                   inc=VC)
        v = r.emit("vector", [("fg", vg2), (VC, v)],
                   lambda e, acc=acc, tt=tt: e.scalar_tensor_tensor(
                       acc, b_g2[:, :], s_w12[:, 2 * tt + 1:2 * tt + 2], acc,
                       ALU.mult, ALU.add), inc=VC)
        v = r.emit("vector", [("xr", vxr), (VC, v)],
                   lambda e, acc=acc: e.tensor_add(acc, acc, b_xr[:, :]), inc=VC)
        v_acc.append(v)
    v_acc_all = r.val(VC)

    ow_free = [0, 0]
    ows = ["owa", "owb"]
    att_done = [0] * 8
    att_mm = [0] * 8
    for dc in range(4):
        for k in range(16):
            bufi = k % 2
            vw = dma("scalar", ows[bufi], s_ow[:, bufi * 512:(bufi + 1) * 512],
                     outwT[k * 128:(k + 1) * 128, dc * 512:(dc + 1) * 512],
                     deps=[(TC, ow_free[bufi])] + (barF if (dc == 0 and k < 2) else []))
            for tt in range(NTT):
                deps = [(ows[bufi], vw), ("zg", v_zg[k])]
                if k == 0:
                    deps.append((VC, att_done[tt]))
                att_mm[tt] = r.emit("tensor", deps,
                                    lambda e, k=k, tt=tt, bufi=bufi: e.matmul(
                                        p_att[tt][:, :],
                                        zA[:, k * FD + tt * 128:k * FD + (tt + 1) * 128],
                                        s_ow[:, bufi * 512:(bufi + 1) * 512],
                                        start=(k == 0), stop=(k == 15)), inc=TC)
            ow_free[bufi] = att_mm[-1]
        for tt in range(NTT):
            acc = b_acc[:, tt * 2048 + dc * 512:tt * 2048 + (dc + 1) * 512]
            v = r.emit("vector", [(TC, att_mm[tt]), (VC, v_acc_all)],
                       lambda e, acc=acc, tt=tt: e.tensor_add(acc, acc,
                                                              p_att[tt][:, :]), inc=VC)
            att_done[tt] = v
            if dc == 3:
                dma("sync", "od", out_tok[tt * 128:(tt + 1) * 128, :],
                    b_acc[:, tt * 2048:(tt + 1) * 2048], deps=[(VC, v)])
    r.wait_only("gpsimd", [("od", r.val("od"))])
    fin_scope.close()

    with nc.Block() as block:
        @block.sync
        def _(eng):
            r.replay("sync", eng)

        @block.tensor
        def _(eng):
            r.replay("tensor", eng)

        @block.vector
        def _(eng):
            r.replay("vector", eng)

        @block.scalar
        def _(eng):
            r.replay("scalar", eng)

        @block.gpsimd
        def _(eng):
            r.replay("gpsimd", eng)

    stack.close()
    return nc


def kernel(L0_attn_x, L0_attn_norm_w, L0_attn_norm_b, L0_attn_x_proj_w,
           L0_attn_dt_proj_w, L0_attn_dt_proj_b, L0_attn_A_log, L0_attn_D_param,
           L0_attn_out_proj_w, L0_ffn_router_w, L0_ffn_w1, L0_ffn_w2):
    x = np.ascontiguousarray(np.asarray(L0_attn_x, dtype=np.float32).reshape(L, D))
    xT = np.ascontiguousarray(x.T)
    xpwT_h = np.ascontiguousarray(np.asarray(L0_attn_x_proj_w, np.float32).T)
    routT_h = np.ascontiguousarray(np.asarray(L0_ffn_router_w, np.float32).T)
    dtw = np.asarray(L0_attn_dt_proj_w, np.float32)
    dtb = np.asarray(L0_attn_dt_proj_b, np.float32)
    alog = np.asarray(L0_attn_A_log, np.float32)
    dpar = np.asarray(L0_attn_D_param, np.float32)
    nw = np.asarray(L0_attn_norm_w, np.float32)
    nb = np.asarray(L0_attn_norm_b, np.float32)
    outwT_h = np.ascontiguousarray(np.asarray(L0_attn_out_proj_w, np.float32).T)
    w1 = np.asarray(L0_ffn_w1, np.float32)
    w2 = np.asarray(L0_ffn_w2, np.float32)

    w1r = w1.reshape(E, 2, 11, 128, 16, 128)
    w1q_h = np.ascontiguousarray(w1r.transpose(0, 2, 5, 1, 4, 3))
    w2r = w2.reshape(E, 4, 512, 11, 128)
    w2q_h = np.ascontiguousarray(w2r.transpose(0, 1, 3, 4, 2))

    ident_h = np.eye(128, dtype=np.float32)
    triu_h = np.triu(np.ones((128, 128), np.float32))
    ones_h = np.ones((128, 128), np.float32)
    ceoff_h = np.ascontiguousarray(np.broadcast_to(
        (np.arange(E, dtype=np.float32) * CAP - 1.0)[None, :], (128, E)))
    tokid_h = np.ascontiguousarray(np.arange(TLOC, dtype=np.int32).reshape(NTT, 128).T)
    nwk_h = np.ascontiguousarray(nw.reshape(16, 128).T)
    nbk_h = np.ascontiguousarray(nb.reshape(16, 128).T)

    nc = build_program()

    in_maps = []
    d_ids = np.arange(D, dtype=np.int32)
    for i in range(NCORES):
        dsl = slice(i * DSL, (i + 1) * DSL)
        tsl = slice(i * TLOC, (i + 1) * TLOC)
        zw_rows = (d_ids // DSL) * (DSL * NCORES) + (d_ids % DSL) * NCORES + i
        zw = np.ascontiguousarray(zw_rows.astype(np.int32).reshape(16, 128).T)
        in_maps.append({
            "x_tok": np.ascontiguousarray(x[tsl, :]),
            "x_dsl": np.ascontiguousarray(xT[dsl, :]),
            "xpwT": xpwT_h,
            "routT": routT_h,
            "dtwT": np.ascontiguousarray(dtw.T[:, dsl]),
            "dtb2": np.ascontiguousarray(dtb[dsl].reshape(2, 128).T),
            "alog2": np.ascontiguousarray(
                alog[dsl].reshape(2, 128, NST).transpose(1, 0, 2).reshape(128, 2 * NST)),
            "dcol2": np.ascontiguousarray(dpar[dsl].reshape(2, 128).T),
            "nwsl": np.ascontiguousarray(nw[dsl].reshape(2, 128).T),
            "nbsl": np.ascontiguousarray(nb[dsl].reshape(2, 128).T),
            "nwk": nwk_h,
            "nbk": nbk_h,
            "outwT": outwT_h,
            "w1q": w1q_h,
            "w2q": w2q_h,
            "ident": ident_h,
            "triuT": triu_h,
            "onesq": ones_h,
            "ceoff": ceoff_h,
            "tokid": tokid_h,
            "zwidx": zw,
        })

    res = run_bass_kernel_spmd(nc, in_maps, core_ids=list(range(NCORES)))
    out = np.concatenate([res.results[i]["out_tok"] for i in range(NCORES)], axis=0)
    return out.reshape(1, L, D).astype(np.float32)
